# version 04
# baseline (speedup 1.0000x reference)
import math
from contextlib import ExitStack

import numpy as np
import ml_dtypes

import concourse.bass as bass
import concourse.mybir as mybir
from concourse.bass_utils import run_bass_kernel_spmd
from concourse.aot_env import allow_ici_all_to_all

F32 = mybir.dt.float32
BF16 = mybir.dt.bfloat16
AF = mybir.ActivationFunctionType
ALU = mybir.AluOpType
AX = mybir.AxisListType

D = 2048
QP = 512
NH = 16
DH = 128
BAT = 2
S = 4096
CHT = 256
T = 1024
NT = 8
CCD = 16
E = 8
I1 = 1408
NIC = 11
IS = 2816
NISC = 22
EPS = 1e-5
ISC = 1.0 / math.sqrt(128.0)
NEGBIG = -1e30
NCORE = 8
HG = 4
QBLK = 786432
OBLK = 262144

F1 = 1.0
FNEG1 = -1.0


class Builder:
    def __init__(self, dbg=()):
        self.nc = bass.Bass("TRN2", target_bir_lowering=False, debug=False)
        self.es = ExitStack()
        self.dbg_names = set(dbg)
        self.dbg_tensors = {}
        nc = self.nc
        self.eng = {
            "t": nc.tensor, "v": nc.vector, "s": nc.scalar,
            "g": nc.gpsimd, "y": nc.sync,
        }
        self.sem = {}
        for name in ("pt", "pv", "ps", "pg", "dy", "dg", "dc", "cc"):
            self.sem[name] = self.es.enter_context(nc.semaphore(name))
        self.cnt = {k: 0 for k in self.sem}
        self.eng_sem = {"t": "pt", "v": "pv", "s": "ps", "g": "pg"}
        self.q_sem = {"y": "dy", "g": "dg", "s": "dc"}
        self.reg = {}
        self.waited = {}

    def _r(self, key):
        if key not in self.reg:
            self.reg[key] = {"w": None, "r": {}}
        return self.reg[key]

    def _wait(self, ename, sem, val):
        if val <= 0:
            return
        assert val <= self.cnt[sem], (ename, sem, val, self.cnt[sem])
        k = (ename, sem)
        if self.waited.get(k, 0) >= val:
            return
        self.eng[ename].wait_ge(self.sem[sem], val)
        self.waited[k] = val

    def pre(self, ename, reads, writes):
        own = self.eng_sem.get(ename)
        for key in reads:
            r = self._r(key)
            if r["w"] is not None:
                sem, c = r["w"]
                if sem == own:
                    self._wait(ename, sem, c)
                else:
                    self._wait(ename, sem, c)
        for key in writes:
            r = self._r(key)
            if r["w"] is not None:
                sem, c = r["w"]
                self._wait(ename, sem, c)
            for sem, c in r["r"].items():
                self._wait(ename, sem, c)

    def post(self, ename, instr, reads, writes, dma=False):
        if dma:
            sem = self.q_sem[ename]
            instr.then_inc(self.sem[sem], 16)
            self.cnt[sem] += 16
            c = self.cnt[sem]
        else:
            sem = self.eng_sem[ename]
            instr.then_inc(self.sem[sem], 1)
            self.cnt[sem] += 1
            c = self.cnt[sem]
        for key in writes:
            r = self._r(key)
            r["w"] = (sem, c)
            r["r"] = {}
        for key in reads:
            r = self._r(key)
            r["r"][sem] = c

    def op(self, ename, fn, reads=(), writes=()):
        self.pre(ename, reads, writes)
        instr = fn(self.eng[ename])
        self.post(ename, instr, reads, writes)

    def dma(self, qname, out, in_, reads=(), writes=(), **kw):
        self.pre(qname, reads, writes)
        instr = self.eng[qname].dma_start(out, in_, **kw)
        self.post(qname, instr, reads, writes, dma=True)

    def dbg(self, name, shape, dtype, src_ap, reads):
        if name not in self.dbg_names:
            return
        t = self.nc.dram_tensor("dbg_" + name, list(shape), dtype,
                                kind="ExternalOutput").ap()
        self.dbg_tensors[name] = t
        self.dma("y", t[:, :] if len(shape) == 2 else t[:], src_ap,
                 reads=reads, writes=("dbg_" + name,))

    def mmgroup(self, pairs, out_ap, reads, writes):
        def fn(t):
            n = len(pairs)
            instr = None
            for i, (lh, rh) in enumerate(pairs):
                instr = t.matmul(out_ap, lh, rh, start=(i == 0), stop=(i == n - 1))
            return instr
        self.op("t", fn, reads=reads, writes=writes)


def ceil_div(a, b):
    return (a + b - 1) // b


def build_program(dbg=(), limit=99):
    b = Builder(dbg)
    nc = b.nc
    es = b.es

    def _cut():
        for k in list(b.reg):
            if k.startswith("dbg_"):
                b.pre("y", (k,), ())
        return b

    def din(name, shape, dtype=F32):
        return nc.dram_tensor(name, list(shape), dtype, kind="ExternalInput").ap()

    x_own = din("x_own", [T, D])
    wdq = din("wdq", [128, CCD * 512])
    wdkv = din("wdkv", [128, CCD * 512])
    wuq = din("wuq", [128, 4 * D])
    wukv = din("wukv", [128, 4 * 2 * D])
    wo = din("wo", [128, CCD * D])
    qg = din("qg", [128, 4]); qb = din("qb", [128, 4])
    kg = din("kg", [128, 4]); kb_ = din("kb", [128, 4])
    gate = din("gate", [128, CCD * 8])
    w1t = din("w1t", [E * NIC, 128, CCD * 128])
    w2t = din("w2t", [E * NIC, 128, CCD * 128])
    w3t = din("w3t", [E * NIC, 128, D])
    sw1t = din("sw1t", [NISC, 128, CCD * 128])
    sw2t = din("sw2t", [NISC, 128, CCD * 128])
    sw3t = din("sw3t", [NISC, 128, D])
    identb = din("identb", [128, 128], BF16)
    identf = din("identf", [128, 128], F32)
    masks = din("masks", [128, 896], F32)
    out = nc.dram_tensor("out", [T, D], F32, kind="ExternalOutput").ap()

    cc_in = nc.dram_tensor("cc_in", [NCORE, QBLK], BF16).ap()
    cc_out = nc.dram_tensor("cc_out", [NCORE, QBLK], BF16).ap()
    cc2_in = nc.dram_tensor("cc2_in", [NCORE, OBLK], BF16).ap()
    cc2_out = nc.dram_tensor("cc2_out", [NCORE, OBLK], BF16).ap()
    x2h = nc.dram_tensor("x2h", [T, D], F32).ap()
    yh = nc.dram_tensor("yh", [T, D], F32).ap()

    def sb(name, shape, dtype=F32):
        return es.enter_context(nc.sbuf_tensor(name, list(shape), dtype))

    ptl = sb("ptl", [128, 2048], BF16)
    scrA = sb("scrA", [128, 512], F32)
    gprb = sb("gprb", [128, 64], F32)

    aB = sb("aB", [128, 67584], BF16)
    aF = sb("aF", [128, 15360], F32)
    idb = sb("idb", [128, 128], BF16)
    idf = sb("idf", [128, 128], F32)
    msk = sb("msk", [128, 896], F32)
    qgb = sb("qgb", [128, 4]); qbb = sb("qbb", [128, 4])
    kgb = sb("kgb", [128, 4]); kbb = sb("kbb", [128, 4])
    gsb = sb("gsb", [128, CCD * 8], BF16)
    onef = sb("onef", [1, 128], F32)
    epsb = sb("epsb", [128, 1], F32)
    stats = sb("stats", [128, 64], F32)
    combr = aF[0:8, 10752:11776]
    lbuf = sb("lbuf", [128, 32], F32)
    o0f = aF[:, 0:1024]

    P0 = es.enter_context(nc.psum_tensor("P0", [128, 2048], F32))
    P1 = es.enter_context(nc.psum_tensor("P1", [128, 512], F32))
    P2 = es.enter_context(nc.psum_tensor("P2", [128, 512], F32))
    PT = es.enter_context(nc.psum_tensor("PT", [128, 2048], BF16))

    HFM = 0
    HTM = 16384
    WDQ = 32768
    WUQo = HTM
    WUKVo = WDQ
    CQF = 49152
    CQN = 57344
    STG = 65536
    PTL = 67584
    KSB = 0
    QSB = 16384
    VSB = 32768
    OSB = 49152
    PTT = 65536
    OTM = 0
    OFM = 16384
    WOB = 32768
    H2TM = 0
    H2FM = 16384
    HEB = 32768
    W12 = 44032
    W3B = 52224
    W3B = 44032
    W12 = 66560
    W12 = 0
    SPARE5 = 8192

    XTM = 0
    SCR = 8192
    XR = 0
    X2T = 4096
    PRB = 10240
    CBC = 8192
    SIL = 9216
    T2R = 12288
    YEV = 13312
    XRL = 0
    OUTR = 4096
    YRL = 15360
    YRL = 8192

    nb = {}

    b.dma("y", idb[:, :], identb[:, :], writes=("idb",))
    b.dma("y", idf[:, :], identf[:, :], writes=("idf",))
    b.dma("y", msk[:, :], masks[:, :], writes=("msk",))
    b.dma("y", qgb[:, :], qg[:, :], writes=("qgb",))
    b.dma("y", qbb[:, :], qb[:, :], writes=("qbb",))
    b.dma("y", kgb[:, :], kg[:, :], writes=("kgb",))
    b.dma("y", kbb[:, :], kb_[:, :], writes=("kbb",))
    b.dma("g", gsb[:, :], gate[:, :], writes=("gsb",))
    b.op("v", lambda v: v.memset(onef[:, :], 1.0), writes=("onef",))
    b.op("v", lambda v: v.memset(epsb[:, :], EPS), writes=("epsb",))
    if limit <= 0.2:
        b.dbg("msk_chk", [128, 896], F32, msk[:, :], reads=("msk",))
        return _cut()

    b.dma("g", aB[:, WDQ:WDQ + 8192], wdq[:, :], writes=("wdqb",))
    b.dma("g", aB[:, WDQ + 8192:WDQ + 16384], wdkv[:, :], writes=("wdkvb",))

    for half in range(2):
        for i in range(4):
            nt = half * 4 + i
            xt = aF[:, XTM + i * 2048:XTM + (i + 1) * 2048]
            b.dma("y", xt, x_own[nt * 128:(nt + 1) * 128, :],
                  writes=(f"xtile{nt}",) + ((f"xtile{nt - 4}",) if half else ()))
        for i in range(4):
            nt = half * 4 + i
            xt = aF[:, XTM + i * 2048:XTM + (i + 1) * 2048]
            ssl = stats[:, nt:nt + 1]
            sql = stats[:, 8 + nt:8 + nt + 1]
            b.op("v", lambda v, xt=xt, ssl=ssl: v.reduce_sum(ssl, xt, axis=AX.X),
                 reads=(f"xtile{nt}",), writes=(f"s1:{nt}",))
            b.op("s", lambda s, xt=xt, sql=sql: s.activation(
                aF[:, SCR:SCR + 2048], xt, AF.Square, accum_out=sql),
                 reads=(f"xtile{nt}",), writes=(f"sq1:{nt}", "scr"))
        for i in range(4):
            nt = half * 4 + i
            ssl = stats[:, nt:nt + 1]
            sql = stats[:, 8 + nt:8 + nt + 1]
            nmu = stats[:, 16 + nt:16 + nt + 1]
            rr = stats[:, 24 + nt:24 + nt + 1]
            b.op("v", lambda v, ssl=ssl, nmu=nmu: v.tensor_scalar(
                nmu, ssl, (-1.0 / D), None, ALU.mult),
                 reads=(f"s1:{nt}",), writes=(f"nmu1:{nt}",))
            b.op("v", lambda v, sql=sql, rr=rr: v.tensor_scalar(
                rr, sql, (1.0 / D), None, ALU.mult),
                 reads=(f"sq1:{nt}",), writes=(f"r1:{nt}",))
            b.op("v", lambda v, nmu=nmu, rr=rr: v.scalar_tensor_tensor(
                rr, nmu, nmu, rr, ALU.mult, ALU.subtract),
                 reads=(f"nmu1:{nt}", f"r1:{nt}"), writes=(f"r1:{nt}",))
            b.op("s", lambda s, rr=rr: s.activation(
                rr, rr, AF.Sqrt, bias=epsb[:, 0:1], scale=FNEG1),
                 reads=(f"r1:{nt}", "epsb"), writes=(f"r1:{nt}",))
            b.op("v", lambda v, rr=rr: v.reciprocal(rr, rr),
                 reads=(f"r1:{nt}",), writes=(f"r1:{nt}",))
        for i in range(4):
            nt = half * 4 + i
            xt = aF[:, XTM + i * 2048:XTM + (i + 1) * 2048]
            ht = aB[:, HTM + nt * 2048:HTM + (nt + 1) * 2048]
            nmu = stats[:, 16 + nt:16 + nt + 1]
            rr = stats[:, 24 + nt:24 + nt + 1]
            b.op("v", lambda v, ht=ht, xt=xt, nmu=nmu, rr=rr: v.tensor_scalar(
                ht, xt, nmu, rr, ALU.add, ALU.mult),
                 reads=(f"xtile{nt}", f"nmu1:{nt}", f"r1:{nt}"),
                 writes=(f"htm{nt}",))
    if limit <= 0.4:
        b.dbg("htm_chk", [128, 16384], BF16, aB[:, HTM:HTM + 16384],
              reads=tuple(f"htm{nt}" for nt in range(NT)))
        return _cut()

    for cc in range(CCD):
        ph = cc % 2
        pts = PT[:, ph * 1024:(ph + 1) * 1024]
        def tfn(t, cc=cc, pts=pts):
            instr = None
            for nt2 in range(NT):
                instr = t.transpose(
                    pts[:, nt2 * 128:(nt2 + 1) * 128],
                    aB[:, HTM + nt2 * 2048 + cc * 128:HTM + nt2 * 2048 + (cc + 1) * 128],
                    idb[:, :])
            return instr
        b.op("t", tfn,
             reads=tuple(f"htm{nt2}" for nt2 in range(NT)) + ("idb",),
             writes=(f"PTh{ph}",))
        hf = aB[:, HFM + cc * 1024:HFM + (cc + 1) * 1024]
        b.op("v", lambda v, hf=hf, pts=pts: v.tensor_copy(hf, pts),
             reads=(f"PTh{ph}",), writes=(f"hfm{cc}",))
    b.dbg("h_fm", [128, 16384], BF16, aB[:, HFM:HFM + 16384],
          reads=tuple(f"hfm{c}" for c in range(CCD)))
    if limit <= 0.6:
        return _cut()

    hfm_reads = tuple(f"hfm{c}" for c in range(CCD))
    for proj, wbkey, woff, gbuf, bbuf, cqnoff, tag in (
        ("q", "wdqb", WDQ, qgb, qbb, CQN, "cq"),
        ("k", "wdkvb", WDQ + 8192, kgb, kbb, CQN + 4096, "ckv"),
    ):
        for nt in range(NT):
            acc = P1 if nt % 2 == 0 else P2
            acck = ("P1" if nt % 2 == 0 else "P2")
            pairs = []
            for kc in range(CCD):
                lh = aB[:, HFM + kc * 1024 + nt * 128:HFM + kc * 1024 + nt * 128 + 128]
                rh = aB[:, woff + kc * 512:woff + (kc + 1) * 512]
                pairs.append((lh, rh))
            b.mmgroup(pairs, acc[:, :], reads=hfm_reads + (wbkey,), writes=(acck,))
            if limit <= 0.7 and proj == "q" and nt == 0:
                b.op("v", lambda v, acc=acc: v.tensor_copy(
                    aF[:, SCR:SCR + 512], acc[:, :]),
                     reads=(acck,), writes=("scr",))
                b.dbg("cqpre0", [128, 512], F32, aF[:, SCR:SCR + 512],
                      reads=("scr",))
                return _cut()
            base = 32 + (nt % 2) * 16
            bn6 = stats[:, base:base + 6]
            mv = stats[:, base + 6:base + 8]
            rr = stats[:, base + 8:base + 9]
            b.op("v", lambda v, bn6=bn6, acc=acc: v.bn_stats(bn6, acc[:, :]),
                 reads=(acck,), writes=(f"bn:{tag}{nt}",))
            b.op("v", lambda v, mv=mv, bn6=bn6: v.bn_aggr(mv, bn6),
                 reads=(f"bn:{tag}{nt}",), writes=(f"mv:{tag}{nt}",))
            b.op("s", lambda s, rr=rr, mv=mv: s.activation(
                rr, mv[:, 1:2], AF.Sqrt, bias=epsb[:, 0:1], scale=F1),
                 reads=(f"mv:{tag}{nt}", "epsb"), writes=(f"r2:{tag}{nt}",))
            b.op("v", lambda v, rr=rr: v.reciprocal(rr, rr),
                 reads=(f"r2:{tag}{nt}",), writes=(f"r2:{tag}{nt}",))
            cqn = aB[:, cqnoff + nt * 512:cqnoff + (nt + 1) * 512]
            b.op("v", lambda v, cqn=cqn, acc=acc, mv=mv, rr=rr: v.tensor_scalar(
                cqn, acc[:, :], mv[:, 0:1], rr, ALU.subtract, ALU.mult),
                 reads=(acck, f"mv:{tag}{nt}", f"r2:{tag}{nt}"),
                 writes=(f"{tag}n{nt}", acck))
            if limit <= 0.8 and proj == "q" and nt == 0:
                b.dbg("stat_chk", [128, 64], F32, stats[:, :],
                      reads=(f"mv:{tag}0",))
                b.dbg("cqn0", [128, 512], BF16, cqn, reads=(f"{tag}n{nt}",))
                return _cut()
        fmoff = CQF if tag == "cq" else CQF + 4096
        for q2 in range(4):
            ph = q2 % 2
            pts = PT[:, ph * 1024:(ph + 1) * 1024]
            def tfn(t, q2=q2, pts=pts, cqnoff=cqnoff):
                instr = None
                for nt2 in range(NT):
                    instr = t.transpose(
                        pts[:, nt2 * 128:(nt2 + 1) * 128],
                        aB[:, cqnoff + nt2 * 512 + q2 * 128:cqnoff + nt2 * 512 + (q2 + 1) * 128],
                        idb[:, :])
                return instr
            b.op("t", tfn,
                 reads=tuple(f"{tag}n{nt2}" for nt2 in range(NT)) + ("idb",),
                 writes=(f"PTh{ph}",))
            dst = aB[:, fmoff + q2 * 1024:fmoff + (q2 + 1) * 1024]
            gb = gbuf[:, q2:q2 + 1]
            bb2 = bbuf[:, q2:q2 + 1]
            gkey = "qgb" if tag == "cq" else "kgb"
            bkey = "qbb" if tag == "cq" else "kbb"
            b.op("v", lambda v, dst=dst, pts=pts, gb=gb, bb2=bb2: v.tensor_scalar(
                dst, pts, gb, bb2, ALU.mult, ALU.add),
                 reads=(f"PTh{ph}", gkey, bkey), writes=(f"{tag}fm{q2}",))
    b.dbg("cq_fm", [128, 4096], BF16, aB[:, CQF:CQF + 4096],
          reads=tuple(f"cqfm{q2}" for q2 in range(4)))
    b.dbg("ckv_fm", [128, 4096], BF16, aB[:, CQF + 4096:CQF + 8192],
          reads=tuple(f"ckvfm{q2}" for q2 in range(4)))
    if limit <= 1:
        return _cut()

    b.dma("g", aB[:, WUQo:WUQo + 8192], wuq[:, :],
          writes=("wuqb",) + tuple(f"htm{nt}" for nt in range(NT)))
    b.dma("g", aB[:, WUKVo:WUKVo + 16384], wukv[:, :],
          writes=("wukvb", "wdqb", "wdkvb"))

    cqfm_reads = tuple(f"cqfm{q2}" for q2 in range(4))
    ckvfm_reads = tuple(f"ckvfm{q2}" for q2 in range(4))

    stg_state = {"i": 0}

    def stage_out(psum_ap, psum_key, scale, dst_ap, dkey):
        sl = stg_state["i"] % 4
        stg_state["i"] += 1
        st = aB[:, STG + sl * 512:STG + (sl + 1) * 512]
        skey = f"stg{sl}"
        if scale is None:
            b.op("v", lambda v, st=st, p=psum_ap: v.tensor_copy(st, p),
                 reads=(psum_key, skey), writes=(skey, psum_key))
        else:
            b.op("v", lambda v, st=st, p=psum_ap: v.tensor_scalar(
                st, p, scale, None, ALU.mult),
                 reads=(psum_key, skey), writes=(skey, psum_key))
        b.dma("y", dst_ap, st, reads=(skey,), writes=(dkey,))

    for oc in range(CCD):
        for ng in range(2):
            acc = P1 if (oc * 2 + ng) % 2 == 0 else P2
            acck = "P1" if (oc * 2 + ng) % 2 == 0 else "P2"
            pairs = []
            for kc in range(4):
                lh = aB[:, WUKVo + kc * 4096 + oc * 128:WUKVo + kc * 4096 + (oc + 1) * 128]
                rh = aB[:, CQF + 4096 + kc * 1024 + ng * 512:CQF + 4096 + kc * 1024 + (ng + 1) * 512]
                pairs.append((lh, rh))
            b.mmgroup(pairs, acc[:, :], reads=ckvfm_reads + ("wukvb",), writes=(acck,))
            d = ng * 4 + oc // 4
            off = (oc % 4) * 128 * 512
            dst = cc_in[d, off:off + 128 * 512].rearrange("(p n) -> p n", p=128)
            stage_out(acc[:, :], acck, None, dst, f"ccinK{d}:{oc % 4}")
    for nt in range(NT):
        for og in range(4):
            acc = P1 if (nt * 4 + og) % 2 == 0 else P2
            acck = "P1" if (nt * 4 + og) % 2 == 0 else "P2"
            pairs = []
            for kc in range(4):
                lh = aB[:, CQF + 4096 + kc * 1024 + nt * 128:CQF + 4096 + kc * 1024 + nt * 128 + 128]
                rh = aB[:, WUKVo + kc * 4096 + 2048 + og * 512:WUKVo + kc * 4096 + 2048 + (og + 1) * 512]
                pairs.append((lh, rh))
            b.mmgroup(pairs, acc[:, :], reads=ckvfm_reads + ("wukvb",), writes=(acck,))
            ci = nt // 2
            d = (ci // 2) * 4 + og
            roff = (ci % 2) * 256 + (nt % 2) * 128
            off = 512 * 512 + roff * 512
            dst = cc_in[d, off:off + 128 * 512].rearrange("(p n) -> p n", p=128)
            stage_out(acc[:, :], acck, None, dst, f"ccinV{d}:{roff // 128}")
    for oc in range(CCD):
        for ng in range(2):
            acc = P1 if (oc * 2 + ng) % 2 == 0 else P2
            acck = "P1" if (oc * 2 + ng) % 2 == 0 else "P2"
            pairs = []
            for kc in range(4):
                lh = aB[:, WUQo + kc * 2048 + oc * 128:WUQo + kc * 2048 + (oc + 1) * 128]
                rh = aB[:, CQF + kc * 1024 + ng * 512:CQF + kc * 1024 + (ng + 1) * 512]
                pairs.append((lh, rh))
            b.mmgroup(pairs, acc[:, :], reads=cqfm_reads + ("wuqb",), writes=(acck,))
            d = ng * 4 + oc // 4
            off = 2 * 512 * 512 + (oc % 4) * 128 * 512
            dst = cc_in[d, off:off + 128 * 512].rearrange("(p n) -> p n", p=128)
            stage_out(acc[:, :], acck, ISC, dst, f"ccinQ{d}:{oc % 4}")

    ccin_keys = tuple(f"ccinK{d}:{i}" for d in range(8) for i in range(4)) + \
        tuple(f"ccinV{d}:{i}" for d in range(8) for i in range(4)) + \
        tuple(f"ccinQ{d}:{i}" for d in range(8) for i in range(4))
    b.pre("g", ccin_keys, ("ccout",))
    cc_instr = nc.gpsimd.collective_compute(
        "AllToAll", ALU.bypass, replica_groups=[list(range(NCORE))],
        ins=[cc_in[:, :]], outs=[cc_out[:, :]])
    cc_instr.then_inc(b.sem["cc"], 1)
    b.cnt["cc"] += 1
    r = b._r("ccout"); r["w"] = ("cc", b.cnt["cc"]); r["r"] = {}
    for k in ccin_keys:
        b._r(k)["r"]["cc"] = b.cnt["cc"]

    for j in range(NCORE):
        blk = cc_out[j, :]
        for which, base, tag2 in (("K", 0, "ksb"), ("Q", 2 * 512 * 512, "qsb")):
            for hc in range(4):
                for half in range(2):
                    src = blk[base + hc * 128 * 512 + half * 256:
                              base + (hc * 128 + 127) * 512 + half * 256 + 256]
                    src2 = blk[base:base + 512 * 512].rearrange(
                        "(d t) -> d t", t=512)[hc * 128:(hc + 1) * 128,
                                               half * 256:(half + 1) * 256]
                    p = (j if half == 0 else 15 - j)
                    dsto = (KSB if which == "K" else QSB) + hc * 4096 + p * 256
                    b.dma("s", aB[:, dsto:dsto + 256], src2,
                          reads=("ccout",), writes=(f"{tag2}{hc}:{p}",))
        for half in range(2):
            for sub in range(2):
                src2 = blk[512 * 512:2 * 512 * 512].rearrange(
                    "(t d) -> t d", d=512)[half * 256 + sub * 128:
                                           half * 256 + (sub + 1) * 128, :]
                p = (j if half == 0 else 15 - j)
                vt = p * 2 + sub
                b.dma("s", aB[:, VSB + vt * 512:VSB + (vt + 1) * 512], src2,
                      reads=("ccout",), writes=(f"vsb{vt}",))

    if limit <= 2:
        b.dbg("ksb", [128, 16384], BF16, aB[:, KSB:KSB + 16384],
              reads=tuple(f"ksb{hc}:{p}" for hc in range(4) for p in range(16)))
        b.dbg("qsb", [128, 16384], BF16, aB[:, QSB:QSB + 16384],
              reads=tuple(f"qsb{hc}:{p}" for hc in range(4) for p in range(16)))
        b.dbg("vsb", [128, 16384], BF16, aB[:, VSB:VSB + 16384],
              reads=tuple(f"vsb{vt}" for vt in range(32)))
        return _cut()

    for qc in range(16):
        nkb = ceil_div(qc + 1, 2)
        passes = [(0, min(4, nkb))]
        if nkb > 4:
            passes.append((4, nkb))
        for qt in range(2):
            qcol = qc * 256 + qt * 128
            for hh in range(4):
                qread = (f"qsb{hh}:{p}" for p in range(qc + 1))
                lh = aB[:, QSB + hh * 4096 + qcol:QSB + hh * 4096 + qcol + 128]
                npass = len(passes)
                for pi, (k0, k1) in enumerate(passes):
                    nk = k1 - k0
                    kreads = tuple(f"ksb{hh}:{p}" for p in range(2 * k0, 2 * k1))
                    def sfn(t, lh=lh, k0=k0, k1=k1, hh=hh):
                        instr = None
                        for kk in range(k1 - k0):
                            rh = aB[:, KSB + hh * 4096 + (k0 + kk) * 512:
                                    KSB + hh * 4096 + (k0 + kk + 1) * 512]
                            instr = t.matmul(P0[:, kk * 512:(kk + 1) * 512], lh, rh,
                                             start=True, stop=True)
                        return instr
                    p0keys = tuple(f"P0:{i}" for i in range(nk))
                    b.op("t", sfn, reads=kreads + (f"qsb{hh}:{qc}",),
                         writes=p0keys)
                    if pi == npass - 1:
                        doff = (0 if qc % 2 == 0 else 256) + qt * 128
                        b.op("v", lambda v, nk=nk, doff=doff: v.tensor_tensor(
                            P0[:, (nk - 1) * 512:nk * 512],
                            P0[:, (nk - 1) * 512:nk * 512],
                            msk[:, 384 - doff:896 - doff], ALU.add),
                             reads=(f"P0:{nk-1}", "msk"), writes=(f"P0:{nk-1}",))
                    nm = lbuf[:, pi:pi + 1]
                    b.op("v", lambda v, nm=nm, nk=nk: v.reduce_max(
                        nm, P0[:, :nk * 512], axis=AX.X, negate=True),
                         reads=p0keys, writes=(f"nm{pi}",))
                    ll = lbuf[:, 4 + pi:5 + pi]
                    b.op("s", lambda s, nm=nm, ll=ll, nk=nk: s.activation(
                        ptl[:, 0:nk * 512], P0[:, :nk * 512], AF.Exp,
                        bias=nm, scale=F1, accum_out=ll),
                         reads=p0keys + (f"nm{pi}",), writes=("ptl", f"l{pi}") + p0keys)
                    nkt = nk * 4
                    for g0 in range(0, nkt, 8):
                        g1 = min(g0 + 8, nkt)
                        ph = (g0 // 8) % 2
                        pts = PT[:, ph * 1024:(ph + 1) * 1024]
                        def tfn(t, g0=g0, g1=g1, pts=pts):
                            instr = None
                            for kt in range(g0, g1):
                                instr = t.transpose(
                                    pts[:, (kt - g0) * 128:(kt - g0 + 1) * 128],
                                    ptl[:, kt * 128:(kt + 1) * 128],
                                    idb[:, :])
                            return instr
                        b.op("t", tfn, reads=("ptl", "idb"), writes=(f"PTh{ph}",))
                        b.op("v", lambda v, g0=g0, g1=g1, pts=pts: v.tensor_copy(
                            aB[:, PTT + g0 * 128:PTT + g1 * 128],
                            pts[:, :(g1 - g0) * 128]),
                             reads=(f"PTh{ph}",), writes=("ptt",))
                    vreads = tuple(f"vsb{vt}" for vt in range(4 * k0, 4 * k1))
                    def pvfn(t, k0=k0, k1=k1, hh=hh, nkt=nkt):
                        instr = None
                        n = nkt
                        for kt in range(n):
                            vt = k0 * 4 + kt
                            lh2 = aB[:, PTT + kt * 128:PTT + (kt + 1) * 128]
                            rh2 = aB[:, VSB + vt * 512 + hh * 128:VSB + vt * 512 + (hh + 1) * 128]
                            instr = t.matmul(P1[:, 0:128], lh2, rh2,
                                             start=(kt == 0), stop=(kt == n - 1))
                        return instr
                    b.op("t", pvfn, reads=("ptt",) + vreads, writes=("P1",))
                    if npass == 2 and pi == 0:
                        o0 = o0f[:, hh * 256 + qt * 128:hh * 256 + (qt + 1) * 128]
                        b.op("v", lambda v, o0=o0: v.tensor_copy(o0, P1[:, 0:128]),
                             reads=("P1",), writes=("o0f", "P1"))
                otile = qc * 2 + qt
                odst = aB[:, OSB + otile * 512 + hh * 128:OSB + otile * 512 + (hh + 1) * 128]
                if npass == 1:
                    rl = lbuf[:, 8:9]
                    b.op("v", lambda v, rl=rl: v.reciprocal(rl, lbuf[:, 4:5]),
                         reads=("l0",), writes=("rl",))
                    b.op("v", lambda v, odst=odst, rl=rl: v.tensor_scalar(
                        odst, P1[:, 0:128], rl, None, ALU.mult),
                         reads=("P1", "rl"), writes=(f"osb{otile}:{hh}", "P1"))
                else:
                    nmf = lbuf[:, 9:10]; a0 = lbuf[:, 10:11]; a1 = lbuf[:, 11:12]
                    lf = lbuf[:, 12:13]; rl = lbuf[:, 8:9]
                    t2c = lbuf[:, 13:14]
                    b.op("v", lambda v, nmf=nmf: v.tensor_tensor(
                        nmf, lbuf[:, 0:1], lbuf[:, 1:2], ALU.min),
                         reads=("nm0", "nm1"), writes=("nmf",))
                    b.op("s", lambda s, a0=a0, nmf=nmf: s.activation(
                        a0, lbuf[:, 0:1], AF.Exp, bias=nmf, scale=FNEG1),
                         reads=("nm0", "nmf"), writes=("a0",))
                    b.op("s", lambda s, a1=a1, nmf=nmf: s.activation(
                        a1, lbuf[:, 1:2], AF.Exp, bias=nmf, scale=FNEG1),
                         reads=("nm1", "nmf"), writes=("a1",))
                    b.op("v", lambda v, t2c=t2c, a1=a1: v.tensor_tensor(
                        t2c, lbuf[:, 5:6], a1, ALU.mult),
                         reads=("l1", "a1"), writes=("t2c",))
                    b.op("v", lambda v, lf=lf, a0=a0, t2c=t2c: v.scalar_tensor_tensor(
                        lf, lbuf[:, 4:5], a0, t2c, ALU.mult, ALU.add),
                         reads=("l0", "a0", "t2c"), writes=("lf",))
                    b.op("v", lambda v, rl=rl, lf=lf: v.reciprocal(rl, lf),
                         reads=("lf",), writes=("rl",))
                    o0 = o0f[:, hh * 256 + qt * 128:hh * 256 + (qt + 1) * 128]
                    b.op("v", lambda v, a1=a1: v.tensor_scalar(
                        P1[:, 128:256], P1[:, 0:128], a1, None, ALU.mult),
                         reads=("P1", "a1"), writes=("P1",))
                    b.op("v", lambda v, o0=o0, a0=a0: v.scalar_tensor_tensor(
                        P1[:, 256:384], o0, a0, P1[:, 128:256], ALU.mult, ALU.add),
                         reads=("o0f", "a0", "P1"), writes=("P1",))
                    b.op("v", lambda v, odst=odst, rl=rl: v.tensor_scalar(
                        odst, P1[:, 256:384], rl, None, ALU.mult),
                         reads=("P1", "rl"), writes=(f"osb{otile}:{hh}", "P1"))

    if limit <= 3:
        b.dbg("osb", [128, 16384], BF16, aB[:, OSB:OSB + 16384],
              reads=tuple(f"osb{t2}:{hh}" for t2 in range(32) for hh in range(4)))
        return _cut()

    for tile in range(32):
        p = tile // 2
        sub = tile % 2
        d = p if p < 8 else 15 - p
        roff = (0 if p < 8 else 256) + sub * 128
        dst = cc2_in[d, roff * 512:(roff + 128) * 512].rearrange("(p n) -> p n", p=128)
        b.dma("y", dst, aB[:, OSB + tile * 512:OSB + (tile + 1) * 512],
              reads=tuple(f"osb{tile}:{hh}" for hh in range(4)),
              writes=(f"cc2in{tile}",))
    b.pre("g", tuple(f"cc2in{t2}" for t2 in range(32)), ("cc2out",))
    cc2 = nc.gpsimd.collective_compute(
        "AllToAll", ALU.bypass, replica_groups=[list(range(NCORE))],
        ins=[cc2_in[:, :]], outs=[cc2_out[:, :]])
    cc2.then_inc(b.sem["cc"], 1)
    b.cnt["cc"] += 1
    r = b._r("cc2out"); r["w"] = ("cc", b.cnt["cc"]); r["r"] = {}
    for t2 in range(32):
        b._r(f"cc2in{t2}")["r"]["cc"] = b.cnt["cc"]

    for nt in range(NT):
        ci = nt // 2
        bsel = ci // 2
        roff = (ci % 2) * 256 + (nt % 2) * 128
        for hg in range(4):
            s2 = bsel * 4 + hg
            src = cc2_out[s2, roff * 512:(roff + 128) * 512].rearrange(
                "(p n) -> p n", p=128)
            b.dma("y", aB[:, OTM + nt * 2048 + hg * 512:OTM + nt * 2048 + (hg + 1) * 512],
                  src, reads=("cc2out",), writes=(f"otm{nt}",))
    b.dbg("o_tm", [128, 16384], BF16, aB[:, OTM:OTM + 16384],
          reads=tuple(f"otm{nt}" for nt in range(NT)))
    for jc in range(CCD):
        ph = jc % 2
        pts = PT[:, ph * 1024:(ph + 1) * 1024]
        def tfn(t, jc=jc, pts=pts):
            instr = None
            for nt2 in range(NT):
                instr = t.transpose(
                    pts[:, nt2 * 128:(nt2 + 1) * 128],
                    aB[:, OTM + nt2 * 2048 + jc * 128:OTM + nt2 * 2048 + (jc + 1) * 128],
                    idb[:, :])
            return instr
        b.op("t", tfn, reads=tuple(f"otm{nt2}" for nt2 in range(NT)) + ("idb",),
             writes=(f"PTh{ph}",))
        b.op("v", lambda v, jc=jc, pts=pts: v.tensor_copy(
            aB[:, OFM + jc * 1024:OFM + (jc + 1) * 1024], pts),
             reads=(f"PTh{ph}",), writes=(f"ofm{jc}",))
    for jc in range(CCD):
        b.dma("g", aB[:, WOB + jc * 2048:WOB + (jc + 1) * 2048],
              wo[:, jc * 2048:(jc + 1) * 2048],
              reads=(), writes=(f"wob{jc}",))
    ofm_reads = tuple(f"ofm{jc}" for jc in range(CCD))
    for nt in range(NT):
        xr = aF[:, XR + (nt % 2) * 2048:XR + (nt % 2 + 1) * 2048]
        b.dma("y", xr, x_own[nt * 128:(nt + 1) * 128, :],
              writes=(f"xrl{nt}",) + ((f"xrl{nt - 2}",) if nt >= 2 else ()))
        x2t = aF[:, X2T + (nt % 2) * 2048:X2T + (nt % 2 + 1) * 2048]
        for ig in range(4):
            acc = P1 if ig % 2 == 0 else P2
            acck = "P1" if ig % 2 == 0 else "P2"
            pairs = []
            for jc in range(CCD):
                lh = aB[:, OFM + jc * 1024 + nt * 128:OFM + jc * 1024 + nt * 128 + 128]
                rh = aB[:, WOB + jc * 2048 + ig * 512:WOB + jc * 2048 + (ig + 1) * 512]
                pairs.append((lh, rh))
            b.mmgroup(pairs, acc[:, :],
                      reads=ofm_reads + tuple(f"wob{jc}" for jc in range(CCD)),
                      writes=(acck,))
            b.op("v", lambda v, x2t=x2t, acc=acc, ig=ig, xr=xr: v.tensor_tensor(
                x2t[:, ig * 512:(ig + 1) * 512], acc[:, :],
                xr[:, ig * 512:(ig + 1) * 512], ALU.add),
                 reads=(acck, f"xrl{nt}"), writes=(f"x2t{nt % 2}:{ig}", acck))
        x2keys = tuple(f"x2t{nt % 2}:{ig}" for ig in range(4))
        b.dma("y", x2h[nt * 128:(nt + 1) * 128, :], x2t,
              reads=x2keys, writes=(f"x2h{nt}",))
        ssl = stats[:, nt:nt + 1]; sql = stats[:, 8 + nt:8 + nt + 1]
        nmu = stats[:, 16 + nt:16 + nt + 1]; rr = stats[:, 24 + nt:24 + nt + 1]
        b.op("v", lambda v, ssl=ssl, x2t=x2t: v.reduce_sum(ssl, x2t, axis=AX.X),
             reads=x2keys, writes=(f"s4:{nt}",))
        b.op("s", lambda s, sql=sql, x2t=x2t: s.activation(
            aF[:, SCR:SCR + 2048], x2t, AF.Square, accum_out=sql),
             reads=x2keys, writes=(f"sq4:{nt}", "scr"))
        b.op("v", lambda v, ssl=ssl, nmu=nmu: v.tensor_scalar(
            nmu, ssl, (-1.0 / D), None, ALU.mult),
             reads=(f"s4:{nt}",), writes=(f"nmu4:{nt}",))
        b.op("v", lambda v, sql=sql, rr=rr: v.tensor_scalar(
            rr, sql, (1.0 / D), None, ALU.mult),
             reads=(f"sq4:{nt}",), writes=(f"r4:{nt}",))
        b.op("v", lambda v, nmu=nmu, rr=rr: v.scalar_tensor_tensor(
            rr, nmu, nmu, rr, ALU.mult, ALU.subtract),
             reads=(f"nmu4:{nt}", f"r4:{nt}"), writes=(f"r4:{nt}",))
        b.op("s", lambda s, rr=rr: s.activation(
            rr, rr, AF.Sqrt, bias=epsb[:, 0:1], scale=FNEG1),
             reads=(f"r4:{nt}", "epsb"), writes=(f"r4:{nt}",))
        b.op("v", lambda v, rr=rr: v.reciprocal(rr, rr),
             reads=(f"r4:{nt}",), writes=(f"r4:{nt}",))
        ht = aB[:, H2TM + nt * 2048:H2TM + (nt + 1) * 2048]
        b.op("v", lambda v, ht=ht, x2t=x2t, nmu=nmu, rr=rr: v.tensor_scalar(
            ht, x2t, nmu, rr, ALU.add, ALU.mult),
             reads=x2keys + (f"nmu4:{nt}", f"r4:{nt}"),
             writes=(f"h2tm{nt}", f"x2t{nt % 2}:0", f"x2t{nt % 2}:1",
                     f"x2t{nt % 2}:2", f"x2t{nt % 2}:3"))
    if "x2" in b.dbg_names:
        t_x2 = nc.dram_tensor("dbg_x2", [T, D], F32, kind="ExternalOutput").ap()
        b.dbg_tensors["x2"] = t_x2
        b.dma("y", t_x2[:, :], x2h[:, :],
              reads=tuple(f"x2h{nt}" for nt in range(NT)), writes=("dbg_x2",))

    for cc in range(CCD):
        ph = cc % 2
        pts = PT[:, ph * 1024:(ph + 1) * 1024]
        def tfn(t, cc=cc, pts=pts):
            instr = None
            for nt2 in range(NT):
                instr = t.transpose(
                    pts[:, nt2 * 128:(nt2 + 1) * 128],
                    aB[:, H2TM + nt2 * 2048 + cc * 128:H2TM + nt2 * 2048 + (cc + 1) * 128],
                    idb[:, :])
            return instr
        b.op("t", tfn, reads=tuple(f"h2tm{nt2}" for nt2 in range(NT)) + ("idb",),
             writes=(f"PTh{ph}",))
        b.op("v", lambda v, cc=cc, pts=pts: v.tensor_copy(
            aB[:, H2FM + cc * 1024:H2FM + (cc + 1) * 1024], pts),
             reads=(f"PTh{ph}",), writes=(f"h2fm{cc}",))
    h2fm_reads = tuple(f"h2fm{cc}" for cc in range(CCD))
    b.dbg("h2_fm", [128, 16384], BF16, aB[:, H2FM:H2FM + 16384], reads=h2fm_reads)
    if limit <= 4:
        return _cut()

    for nt in range(NT):
        pairs = []
        for kc in range(CCD):
            lh = aB[:, H2FM + kc * 1024 + nt * 128:H2FM + kc * 1024 + nt * 128 + 128]
            rh = gsb[:, kc * 8:(kc + 1) * 8]
            pairs.append((lh, rh))
        b.mmgroup(pairs, P0[:, 0:8], reads=h2fm_reads + ("gsb",), writes=("P0:0",))
        prb = gprb[:, nt * 8:nt * 8 + 8]
        wrk = aF[:, PRB + nt * 64 + 8:PRB + nt * 64 + 16]
        cmb = aF[:, PRB + nt * 64 + 16:PRB + nt * 64 + 24]
        nm = lbuf[:, 16:17]; sm = lbuf[:, 17:18]; rs = lbuf[:, 18:19]
        m1 = lbuf[:, 19:20]; m2 = lbuf[:, 20:21]
        b.op("v", lambda v, nm=nm: v.reduce_max(nm, P0[:, 0:8], axis=AX.X, negate=True),
             reads=("P0:0",), writes=("gnm",))
        b.op("s", lambda s, prb=prb, nm=nm, sm=sm: s.activation(
            prb, P0[:, 0:8], AF.Exp, bias=nm, scale=F1, accum_out=sm),
             reads=("P0:0", "gnm"), writes=(f"gprb{nt}", "gsm", "P0:0"))
        b.op("v", lambda v, rs=rs, sm=sm: v.reciprocal(rs, sm),
             reads=("gsm",), writes=("grs",))
        b.op("v", lambda v, prb=prb, rs=rs: v.tensor_scalar(
            prb, prb, rs, None, ALU.mult),
             reads=(f"gprb{nt}", "grs"), writes=(f"gprb{nt}",))
        b.op("v", lambda v, m1=m1, prb=prb: v.reduce_max(m1, prb, axis=AX.X),
             reads=(f"gprb{nt}",), writes=("gm1",))
        b.op("v", lambda v, wrk=wrk, prb=prb, m1=m1: v.tensor_scalar(
            wrk, prb, m1, None, ALU.is_ge),
             reads=(f"gprb{nt}", "gm1"), writes=(f"gwrk{nt}",))
        b.op("v", lambda v, wrk=wrk, prb=prb: v.scalar_tensor_tensor(
            wrk, wrk, (-1e30), prb, ALU.mult, ALU.add),
             reads=(f"gwrk{nt}", f"gprb{nt}"), writes=(f"gwrk{nt}",))
        b.op("v", lambda v, m2=m2, wrk=wrk: v.reduce_max(m2, wrk, axis=AX.X),
             reads=(f"gwrk{nt}",), writes=("gm2",))
        b.op("v", lambda v, cmb=cmb, prb=prb, m2=m2: v.tensor_scalar(
            cmb, prb, m2, None, ALU.is_ge),
             reads=(f"gprb{nt}", "gm2"), writes=(f"gcmb{nt}",))
        b.op("v", lambda v, cmb=cmb, prb=prb: v.tensor_tensor(
            cmb, cmb, prb, ALU.mult),
             reads=(f"gcmb{nt}", f"gprb{nt}"), writes=(f"gcmb{nt}",))
        b.op("t", lambda t, cmb=cmb, nt=nt: t.transpose(
            P0[0:8, 1024 + nt * 128:1024 + (nt + 1) * 128], cmb, idf[:, :]),
             reads=(f"gcmb{nt}", "idf"), writes=("P0comb",))
    b.op("v", lambda v: v.tensor_copy(combr[:, :], P0[0:8, 1024:2048]),
         reads=("P0comb",), writes=("combr",))
    b.dbg("comb_rows", [8, 1024], F32, combr[:, :], reads=("combr",))
    cflat1 = aF[0:1, 14336:15360]
    if limit <= 5:
        return _cut()

    def expert_pass(eidx, n_ic, w1src, w2src, w3src, use_comb, tagp):
        first = (eidx == 0)
        if use_comb:
            b.dma("y", cflat1[0:1, :], combr[eidx:eidx + 1, :],
                  reads=("combr",), writes=("cflat",))
            def cbfn(t):
                instr = None
                for ng in range(2):
                    instr = t.matmul(P0[:, 1024 + ng * 512:1024 + (ng + 1) * 512],
                                     onef[:, :],
                                     cflat1[0:1, ng * 512:(ng + 1) * 512],
                                     start=True, stop=True)
                return instr
            b.op("t", cbfn, reads=("onef", "cflat"), writes=("P0:2", "P0:3"))
            b.op("v", lambda v: v.tensor_copy(aF[:, CBC:CBC + 1024], P0[:, 1024:2048]),
                 reads=("P0:2", "P0:3"), writes=("cbc", "P0:2", "P0:3"))
        for ic in range(n_ic):
            sl = ic % 2
            w1b = aB[:, W12 + sl * 2048:W12 + (sl + 1) * 2048]
            w2b = aB[:, W12 + 4096 + sl * 2048:W12 + 4096 + (sl + 1) * 2048]
            cl1 = (f"h2tm{sl}",) if (first and ic < 2) else ()
            cl2 = (f"h2tm{2 + sl}",) if (first and ic < 2) else ()
            b.dma("g", w1b, w1src[ic, :, :], writes=(f"w1b{sl}",) + cl1)
            b.dma("g", w2b, w2src[ic, :, :], writes=(f"w2b{sl}",) + cl2)
            for ng in range(2):
                accs = [(P1, "P1"), (P2, "P2")]
                (a1, k1), (a2, k2) = accs
                pairs1 = []
                pairs2 = []
                for kc in range(CCD):
                    lh1 = w1b[:, kc * 128:(kc + 1) * 128]
                    lh2 = w2b[:, kc * 128:(kc + 1) * 128]
                    rh = aB[:, H2FM + kc * 1024 + ng * 512:H2FM + kc * 1024 + (ng + 1) * 512]
                    pairs1.append((lh1, rh))
                    pairs2.append((lh2, rh))
                b.mmgroup(pairs1, a1[:, :], reads=h2fm_reads + (f"w1b{sl}",), writes=(k1,))
                b.mmgroup(pairs2, a2[:, :], reads=h2fm_reads + (f"w2b{sl}",), writes=(k2,))
                sil = aF[:, SIL + ng * 512:SIL + (ng + 1) * 512]
                b.op("s", lambda s, sil=sil, a1=a1: s.activation(sil, a1[:, :], AF.Silu),
                     reads=(k1,), writes=(f"sil{ng}", k1))
                t2 = aF[:, T2R + ng * 512:T2R + (ng + 1) * 512]
                b.op("v", lambda v, t2=t2, a2=a2, sil=sil: v.tensor_tensor(
                    t2, a2[:, :], sil, ALU.mult),
                     reads=(k2, f"sil{ng}"), writes=(f"t2r{ng}", k2))
                hdst = aB[:, HEB + ic * 1024 + ng * 512:HEB + ic * 1024 + (ng + 1) * 512]
                if use_comb:
                    b.op("g", lambda g, hdst=hdst, t2=t2, ng=ng: g.tensor_tensor(
                        hdst, t2, aF[:, CBC + ng * 512:CBC + (ng + 1) * 512], ALU.mult),
                         reads=(f"t2r{ng}", "cbc"), writes=(f"heb{ic}",))
                else:
                    b.op("g", lambda g, hdst=hdst, t2=t2: g.tensor_copy(hdst, t2),
                         reads=(f"t2r{ng}",), writes=(f"heb{ic}",))
        for ic in range(n_ic):
            b.dma("g", aB[:, W3B + ic * 2048:W3B + (ic + 1) * 2048],
                  w3src[ic, :, :], writes=(f"w3b{ic}",))
        heb_reads = tuple(f"heb{ic}" for ic in range(n_ic))
        w3_reads = tuple(f"w3b{ic}" for ic in range(n_ic))
        for nt in range(NT):
            for cg in range(4):
                pslot = (nt * 4 + cg) % 4
                acc = P0[:, pslot * 512:(pslot + 1) * 512]
                acck = f"P0:{pslot}"
                pairs = []
                for ic in range(n_ic):
                    lh = aB[:, HEB + ic * 1024 + nt * 128:HEB + ic * 1024 + nt * 128 + 128]
                    rh = aB[:, W3B + ic * 2048 + cg * 512:W3B + ic * 2048 + (cg + 1) * 512]
                    pairs.append((lh, rh))
                b.mmgroup(pairs, acc, reads=heb_reads + w3_reads, writes=(acck,))
                ysl = (nt * 4 + cg) % 2
                yev = aF[:, YEV + ysl * 512:YEV + (ysl + 1) * 512]
                b.op("v", lambda v, yev=yev, acc=acc: v.tensor_copy(yev, acc),
                     reads=(acck,), writes=(f"yev{ysl}", acck))
                dst = yh[nt * 128:(nt + 1) * 128, cg * 512:(cg + 1) * 512]
                b.dma("g", dst, yev, reads=(f"yev{ysl}",),
                      writes=(f"yh{nt}:{cg}",),
                      accum_op=(ALU.add if eidx > 0 else ALU.bypass))

    for e in range(E):
        expert_pass(e, NIC,
                    w1t[e * NIC:(e + 1) * NIC, :, :],
                    w2t[e * NIC:(e + 1) * NIC, :, :],
                    w3t[e * NIC:(e + 1) * NIC, :, :],
                    use_comb=True, tagp=f"e{e}")
    expert_pass(100, NISC // 2, sw1t[0:11, :, :], sw2t[0:11, :, :], sw3t[0:11, :, :],
                use_comb=False, tagp="sA")
    expert_pass(101, NISC // 2, sw1t[11:22, :, :], sw2t[11:22, :, :], sw3t[11:22, :, :],
                use_comb=False, tagp="sB")

    FX = 0
    FY = 4096
    FO = 8192
    for nt in range(NT):
        sl = nt % 2
        fx = aF[:, FX + sl * 2048:FX + (sl + 1) * 2048]
        fy = aF[:, FY + sl * 2048:FY + (sl + 1) * 2048]
        fo = aF[:, FO + sl * 2048:FO + (sl + 1) * 2048]
        b.dma("y", fx, x2h[nt * 128:(nt + 1) * 128, :],
              reads=(f"x2h{nt}",), writes=(f"fx{sl}",))
        b.dma("y", fy, yh[nt * 128:(nt + 1) * 128, :],
              reads=tuple(f"yh{nt}:{cg}" for cg in range(4)), writes=(f"fy{sl}",))
        b.op("v", lambda v, fo=fo, fx=fx, fy=fy: v.tensor_tensor(
            fo, fx, fy, ALU.add),
             reads=(f"fx{sl}", f"fy{sl}"), writes=(f"fo{sl}", f"fx{sl}", f"fy{sl}"))
        b.dma("y", out[nt * 128:(nt + 1) * 128, :], fo,
              reads=(f"fo{sl}",), writes=(f"outd{nt}",))

    b.pre("y", tuple(f"outd{nt}" for nt in range(NT)), ())
    return b



def _relayout(inputs):
    g = {}
    f32 = np.float32
    g["wdq"] = np.ascontiguousarray(
        inputs["L0_attn_W_dq"].reshape(16, 128, 512).transpose(1, 0, 2).reshape(128, -1))
    g["wdkv"] = np.ascontiguousarray(
        inputs["L0_attn_W_dkv"].reshape(16, 128, 512).transpose(1, 0, 2).reshape(128, -1))
    g["wuq"] = np.ascontiguousarray(
        inputs["L0_attn_W_uq"].reshape(4, 128, 2048).transpose(1, 0, 2).reshape(128, -1))
    g["wukv"] = np.ascontiguousarray(
        inputs["L0_attn_W_ukv"].reshape(4, 128, 4096).transpose(1, 0, 2).reshape(128, -1))
    g["wo"] = np.ascontiguousarray(
        inputs["L0_attn_W_o"].T.reshape(16, 128, 2048).transpose(1, 0, 2).reshape(128, -1))
    g["qg"] = np.ascontiguousarray(inputs["L0_attn_q_ln_g"].reshape(4, 128).T)
    g["qb"] = np.ascontiguousarray(inputs["L0_attn_q_ln_b"].reshape(4, 128).T)
    g["kg"] = np.ascontiguousarray(inputs["L0_attn_kv_ln_g"].reshape(4, 128).T)
    g["kb"] = np.ascontiguousarray(inputs["L0_attn_kv_ln_b"].reshape(4, 128).T)
    g["gate"] = np.ascontiguousarray(
        inputs["L0_ffn_gate_w"].T.reshape(16, 128, 8).transpose(1, 0, 2).reshape(128, -1))

    def tile_w12(w):
        E_, I_, C_ = w.shape
        nic = I_ // 128
        wt = w.transpose(0, 2, 1)
        out = wt.reshape(E_, 16, 128, nic, 128).transpose(0, 3, 2, 1, 4)
        return np.ascontiguousarray(out.reshape(E_ * nic, 128, 16 * 128))

    def tile_w3(w):
        E_, C_, I_ = w.shape
        nic = I_ // 128
        wt = w.transpose(0, 2, 1)
        return np.ascontiguousarray(wt.reshape(E_ * nic, 128, C_))

    g["w1t"] = tile_w12(inputs["L0_ffn_w1"])
    g["w2t"] = tile_w12(inputs["L0_ffn_w2"])
    g["w3t"] = tile_w3(inputs["L0_ffn_w3"])
    g["sw1t"] = tile_w12(inputs["L0_ffn_sw1"][None])
    g["sw2t"] = tile_w12(inputs["L0_ffn_sw2"][None])
    g["sw3t"] = tile_w3(inputs["L0_ffn_sw3"][None])
    g["identb"] = np.eye(128, dtype=ml_dtypes.bfloat16)
    g["identf"] = np.eye(128, dtype=np.float32)
    i = np.arange(128)[:, None]
    c = np.arange(896)[None, :]
    g["masks"] = np.ascontiguousarray(
        np.where(c - 384 <= i, 0.0, NEGBIG).astype(np.float32))
    return g


_BUILD_CACHE = {}


def kernel(**inputs):
    x = inputs["L0_attn_x"]
    g = _relayout(inputs)
    dbg = tuple(sorted(kernel.debug)) if getattr(kernel, "debug", None) else ()
    limit = getattr(kernel, "limit", 99)
    key = (dbg, limit)
    if key not in _BUILD_CACHE:
        with allow_ici_all_to_all():
            _BUILD_CACHE[key] = build_program(dbg, limit)
    b = _BUILD_CACHE[key]
    in_maps = []
    for r in range(NCORE):
        chunks = [x[0, 256 * r:256 * (r + 1)], x[0, 256 * (15 - r):256 * (16 - r)],
                  x[1, 256 * r:256 * (r + 1)], x[1, 256 * (15 - r):256 * (16 - r)]]
        m = dict(g)
        m["x_own"] = np.ascontiguousarray(np.concatenate(chunks, axis=0))
        in_maps.append(m)
    res = run_bass_kernel_spmd(b.nc, in_maps, core_ids=list(range(NCORE)))
    outp = np.zeros((2, 4096, 2048), dtype=np.float32)
    for r in range(NCORE):
        o = res.results[r]["out"]
        outp[0, 256 * r:256 * (r + 1)] = o[0:256]
        outp[0, 256 * (15 - r):256 * (16 - r)] = o[256:512]
        outp[1, 256 * r:256 * (r + 1)] = o[512:768]
        outp[1, 256 * (15 - r):256 * (16 - r)] = o[768:1024]
    kernel.last_results = res
    return outp


kernel.debug = ()


# version 07
# speedup vs baseline: 1.0325x; 1.0325x over previous
import math
from contextlib import ExitStack

import numpy as np
import ml_dtypes

import concourse.bass as bass
import concourse.mybir as mybir
from concourse.bass_utils import run_bass_kernel_spmd
from concourse.aot_env import allow_ici_all_to_all

F32 = mybir.dt.float32
BF16 = mybir.dt.bfloat16
AF = mybir.ActivationFunctionType
ALU = mybir.AluOpType
AX = mybir.AxisListType

D = 2048
QP = 512
NH = 16
DH = 128
BAT = 2
S = 4096
CHT = 256
T = 1024
NT = 8
CCD = 16
E = 8
I1 = 1408
NIC = 11
IS = 2816
NISC = 22
EPS = 1e-5
ISC = 1.0 / math.sqrt(128.0)
NEGBIG = -1e30
NCORE = 8
HG = 4
QBLK = 786432
OBLK = 262144

F1 = 1.0
FNEG1 = -1.0


class Builder:
    def __init__(self, dbg=()):
        self.nc = bass.Bass("TRN2", target_bir_lowering=False, debug=False)
        self.es = ExitStack()
        self.dbg_names = set(dbg)
        self.dbg_tensors = {}
        nc = self.nc
        self.eng = {
            "t": nc.tensor, "v": nc.vector, "s": nc.scalar,
            "g": nc.gpsimd, "y": nc.sync,
        }
        self.sem = {}
        for name in ("pt", "pv", "ps", "pg", "dy", "dg", "dc", "cc"):
            self.sem[name] = self.es.enter_context(nc.semaphore(name))
        self.cnt = {k: 0 for k in self.sem}
        self.eng_sem = {"t": "pt", "v": "pv", "s": "ps", "g": "pg"}
        self.q_sem = {"y": "dy", "g": "dg", "s": "dc"}
        self.reg = {}
        self.waited = {}

    def _r(self, key):
        if key not in self.reg:
            self.reg[key] = {"w": None, "r": {}}
        return self.reg[key]

    def _wait(self, ename, sem, val):
        if val <= 0:
            return
        assert val <= self.cnt[sem], (ename, sem, val, self.cnt[sem])
        k = (ename, sem)
        if self.waited.get(k, 0) >= val:
            return
        self.eng[ename].wait_ge(self.sem[sem], val)
        self.waited[k] = val

    def pre(self, ename, reads, writes):
        own = self.eng_sem.get(ename)
        for key in reads:
            r = self._r(key)
            if r["w"] is not None:
                sem, c = r["w"]
                if sem == own:
                    self._wait(ename, sem, c)
                else:
                    self._wait(ename, sem, c)
        for key in writes:
            r = self._r(key)
            if r["w"] is not None:
                sem, c = r["w"]
                self._wait(ename, sem, c)
            for sem, c in r["r"].items():
                self._wait(ename, sem, c)

    def post(self, ename, instr, reads, writes, dma=False):
        if dma:
            sem = self.q_sem[ename]
            instr.then_inc(self.sem[sem], 16)
            self.cnt[sem] += 16
            c = self.cnt[sem]
        else:
            sem = self.eng_sem[ename]
            instr.then_inc(self.sem[sem], 1)
            self.cnt[sem] += 1
            c = self.cnt[sem]
        for key in writes:
            r = self._r(key)
            r["w"] = (sem, c)
            r["r"] = {}
        for key in reads:
            r = self._r(key)
            r["r"][sem] = c

    def op(self, ename, fn, reads=(), writes=()):
        self.pre(ename, reads, writes)
        instr = fn(self.eng[ename])
        self.post(ename, instr, reads, writes)

    def dma(self, qname, out, in_, reads=(), writes=(), **kw):
        self.pre(qname, reads, writes)
        instr = self.eng[qname].dma_start(out, in_, **kw)
        self.post(qname, instr, reads, writes, dma=True)

    def dbg(self, name, shape, dtype, src_ap, reads):
        if name not in self.dbg_names:
            return
        t = self.nc.dram_tensor("dbg_" + name, list(shape), dtype,
                                kind="ExternalOutput").ap()
        self.dbg_tensors[name] = t
        self.dma("y", t[:, :] if len(shape) == 2 else t[:], src_ap,
                 reads=reads, writes=("dbg_" + name,))

    def mmgroup(self, pairs, out_ap, reads, writes):
        def fn(t):
            n = len(pairs)
            instr = None
            for i, (lh, rh) in enumerate(pairs):
                instr = t.matmul(out_ap, lh, rh, start=(i == 0), stop=(i == n - 1))
            return instr
        self.op("t", fn, reads=reads, writes=writes)


def ceil_div(a, b):
    return (a + b - 1) // b


def build_program(dbg=(), limit=99):
    b = Builder(dbg)
    nc = b.nc
    es = b.es

    def _cut():
        for k in list(b.reg):
            if k.startswith("dbg_"):
                b.pre("y", (k,), ())
        return b

    def din(name, shape, dtype=F32):
        return nc.dram_tensor(name, list(shape), dtype, kind="ExternalInput").ap()

    x_own = din("x_own", [T, D])
    wdq = din("wdq", [128, CCD * 512])
    wdkv = din("wdkv", [128, CCD * 512])
    wuq = din("wuq", [128, 4 * D])
    wukv = din("wukv", [128, 4 * 2 * D])
    wo = din("wo", [128, CCD * D])
    qg = din("qg", [128, 4]); qb = din("qb", [128, 4])
    kg = din("kg", [128, 4]); kb_ = din("kb", [128, 4])
    gate = din("gate", [128, CCD * 8])
    w1t = din("w1t", [E * NIC, 128, CCD * 128])
    w2t = din("w2t", [E * NIC, 128, CCD * 128])
    w3t = din("w3t", [E * NIC, 128, D])
    sw1t = din("sw1t", [NISC, 128, CCD * 128])
    sw2t = din("sw2t", [NISC, 128, CCD * 128])
    sw3t = din("sw3t", [NISC, 128, D])
    identb = din("identb", [128, 128], BF16)
    identf = din("identf", [128, 128], F32)
    masks = din("masks", [128, 896], F32)
    out = nc.dram_tensor("out", [T, D], F32, kind="ExternalOutput").ap()

    cc_in = nc.dram_tensor("cc_in", [NCORE, QBLK], BF16).ap()
    cc_out = nc.dram_tensor("cc_out", [NCORE, QBLK], BF16).ap()
    cc2_in = nc.dram_tensor("cc2_in", [NCORE, OBLK], BF16).ap()
    cc2_out = nc.dram_tensor("cc2_out", [NCORE, OBLK], BF16).ap()
    x2h = nc.dram_tensor("x2h", [T, D], F32).ap()
    yh = nc.dram_tensor("yh", [T, D], F32).ap()

    def sb(name, shape, dtype=F32):
        return es.enter_context(nc.sbuf_tensor(name, list(shape), dtype))

    ptl = sb("ptl", [128, 2048], BF16)
    scrA = sb("scrA", [128, 512], F32)
    gprb = sb("gprb", [128, 64], F32)

    aB = sb("aB", [128, 67584], BF16)
    aF = sb("aF", [128, 15360], F32)
    idb = sb("idb", [128, 128], BF16)
    idf = sb("idf", [128, 128], F32)
    msk = sb("msk", [128, 896], F32)
    qgb = sb("qgb", [128, 4]); qbb = sb("qbb", [128, 4])
    kgb = sb("kgb", [128, 4]); kbb = sb("kbb", [128, 4])
    gsb = sb("gsb", [128, CCD * 8], BF16)
    onef = sb("onef", [1, 128], F32)
    epsb = sb("epsb", [128, 1], F32)
    stats = sb("stats", [128, 64], F32)
    combr = aF[0:8, 10752:11776]
    lbuf = sb("lbuf", [128, 32], F32)
    o0f = aF[:, 0:1024]

    P0 = es.enter_context(nc.psum_tensor("P0", [128, 2048], F32))
    P1 = es.enter_context(nc.psum_tensor("P1", [128, 512], F32))
    P2 = es.enter_context(nc.psum_tensor("P2", [128, 512], F32))
    PT = es.enter_context(nc.psum_tensor("PT", [128, 2048], BF16))

    HFM = 0
    HTM = 16384
    WDQ = 32768
    WUQo = HTM
    WUKVo = WDQ
    CQF = 49152
    CQN = 57344
    STG = 65536
    PTL = 67584
    KSB = 0
    QSB = 16384
    VSB = 32768
    OSB = 49152
    PTT = 65536
    OTM = 0
    OFM = 16384
    WOB = 32768
    H2TM = 0
    H2FM = 16384
    HEB = 32768
    W12 = 44032
    W3B = 52224
    W3B = 44032
    W12 = 66560
    W12 = 0
    SPARE5 = 8192

    XTM = 0
    SCR = 8192
    XR = 0
    X2T = 4096
    PRB = 10240
    CBC = 8192
    SIL = 9216
    T2R = 12288
    YEV = 13312
    XRL = 0
    OUTR = 4096
    YRL = 15360
    YRL = 8192

    nb = {}

    b.dma("y", idb[:, :], identb[:, :], writes=("idb",))
    b.dma("y", idf[:, :], identf[:, :], writes=("idf",))
    b.dma("y", msk[:, :], masks[:, :], writes=("msk",))
    b.dma("y", qgb[:, :], qg[:, :], writes=("qgb",))
    b.dma("y", qbb[:, :], qb[:, :], writes=("qbb",))
    b.dma("y", kgb[:, :], kg[:, :], writes=("kgb",))
    b.dma("y", kbb[:, :], kb_[:, :], writes=("kbb",))
    b.dma("g", gsb[:, :], gate[:, :], writes=("gsb",))
    b.op("v", lambda v: v.memset(onef[:, :], 1.0), writes=("onef",))
    b.op("v", lambda v: v.memset(epsb[:, :], EPS), writes=("epsb",))
    if limit <= 0.2:
        b.dbg("msk_chk", [128, 896], F32, msk[:, :], reads=("msk",))
        return _cut()

    b.dma("g", aB[:, WDQ:WDQ + 8192], wdq[:, :], writes=("wdqb",))
    b.dma("g", aB[:, WDQ + 8192:WDQ + 16384], wdkv[:, :], writes=("wdkvb",))

    for half in range(2):
        for i in range(4):
            nt = half * 4 + i
            xt = aF[:, XTM + i * 2048:XTM + (i + 1) * 2048]
            b.dma("y", xt, x_own[nt * 128:(nt + 1) * 128, :],
                  writes=(f"xtile{nt}",) + ((f"xtile{nt - 4}",) if half else ()))
        for i in range(4):
            nt = half * 4 + i
            xt = aF[:, XTM + i * 2048:XTM + (i + 1) * 2048]
            ssl = stats[:, nt:nt + 1]
            sql = stats[:, 8 + nt:8 + nt + 1]
            b.op("v", lambda v, xt=xt, ssl=ssl: v.reduce_sum(ssl, xt, axis=AX.X),
                 reads=(f"xtile{nt}",), writes=(f"s1:{nt}",))
            b.op("s", lambda s, xt=xt, sql=sql: s.activation(
                aF[:, SCR:SCR + 2048], xt, AF.Square, accum_out=sql),
                 reads=(f"xtile{nt}",), writes=(f"sq1:{nt}", "scr"))
        for i in range(4):
            nt = half * 4 + i
            ssl = stats[:, nt:nt + 1]
            sql = stats[:, 8 + nt:8 + nt + 1]
            nmu = stats[:, 16 + nt:16 + nt + 1]
            rr = stats[:, 24 + nt:24 + nt + 1]
            b.op("v", lambda v, ssl=ssl, nmu=nmu: v.tensor_scalar(
                nmu, ssl, (-1.0 / D), None, ALU.mult),
                 reads=(f"s1:{nt}",), writes=(f"nmu1:{nt}",))
            b.op("v", lambda v, sql=sql, rr=rr: v.tensor_scalar(
                rr, sql, (1.0 / D), None, ALU.mult),
                 reads=(f"sq1:{nt}",), writes=(f"r1:{nt}",))
            b.op("v", lambda v, nmu=nmu, rr=rr: v.scalar_tensor_tensor(
                rr, nmu, nmu, rr, ALU.mult, ALU.subtract),
                 reads=(f"nmu1:{nt}", f"r1:{nt}"), writes=(f"r1:{nt}",))
            b.op("s", lambda s, rr=rr: s.activation(
                rr, rr, AF.Sqrt, bias=epsb[:, 0:1], scale=FNEG1),
                 reads=(f"r1:{nt}", "epsb"), writes=(f"r1:{nt}",))
            b.op("v", lambda v, rr=rr: v.reciprocal(rr, rr),
                 reads=(f"r1:{nt}",), writes=(f"r1:{nt}",))
        for i in range(4):
            nt = half * 4 + i
            xt = aF[:, XTM + i * 2048:XTM + (i + 1) * 2048]
            ht = aB[:, HTM + nt * 2048:HTM + (nt + 1) * 2048]
            nmu = stats[:, 16 + nt:16 + nt + 1]
            rr = stats[:, 24 + nt:24 + nt + 1]
            b.op("v", lambda v, ht=ht, xt=xt, nmu=nmu, rr=rr: v.tensor_scalar(
                ht, xt, nmu, rr, ALU.add, ALU.mult),
                 reads=(f"xtile{nt}", f"nmu1:{nt}", f"r1:{nt}"),
                 writes=(f"htm{nt}",))
    if limit <= 0.4:
        b.dbg("htm_chk", [128, 16384], BF16, aB[:, HTM:HTM + 16384],
              reads=tuple(f"htm{nt}" for nt in range(NT)))
        return _cut()

    for cc in range(CCD):
        ph = cc % 2
        pts = PT[:, ph * 1024:(ph + 1) * 1024]
        def tfn(t, cc=cc, pts=pts):
            instr = None
            for nt2 in range(NT):
                instr = t.transpose(
                    pts[:, nt2 * 128:(nt2 + 1) * 128],
                    aB[:, HTM + nt2 * 2048 + cc * 128:HTM + nt2 * 2048 + (cc + 1) * 128],
                    idb[:, :])
            return instr
        b.op("t", tfn,
             reads=tuple(f"htm{nt2}" for nt2 in range(NT)) + ("idb",),
             writes=(f"PTh{ph}",))
        hf = aB[:, HFM + cc * 1024:HFM + (cc + 1) * 1024]
        b.op("v", lambda v, hf=hf, pts=pts: v.tensor_copy(hf, pts),
             reads=(f"PTh{ph}",), writes=(f"hfm{cc}",))
    b.dbg("h_fm", [128, 16384], BF16, aB[:, HFM:HFM + 16384],
          reads=tuple(f"hfm{c}" for c in range(CCD)))
    if limit <= 0.6:
        return _cut()

    hfm_reads = tuple(f"hfm{c}" for c in range(CCD))
    for proj, wbkey, woff, gbuf, bbuf, cqnoff, tag in (
        ("q", "wdqb", WDQ, qgb, qbb, CQN, "cq"),
        ("k", "wdkvb", WDQ + 8192, kgb, kbb, CQN + 4096, "ckv"),
    ):
        for nt in range(NT):
            acc = P1 if nt % 2 == 0 else P2
            acck = ("P1" if nt % 2 == 0 else "P2")
            pairs = []
            for kc in range(CCD):
                lh = aB[:, HFM + kc * 1024 + nt * 128:HFM + kc * 1024 + nt * 128 + 128]
                rh = aB[:, woff + kc * 512:woff + (kc + 1) * 512]
                pairs.append((lh, rh))
            b.mmgroup(pairs, acc[:, :], reads=hfm_reads + (wbkey,), writes=(acck,))
            if limit <= 0.7 and proj == "q" and nt == 0:
                b.op("v", lambda v, acc=acc: v.tensor_copy(
                    aF[:, SCR:SCR + 512], acc[:, :]),
                     reads=(acck,), writes=("scr",))
                b.dbg("cqpre0", [128, 512], F32, aF[:, SCR:SCR + 512],
                      reads=("scr",))
                return _cut()
            base = 32 + (nt % 2) * 16
            bn6 = stats[:, base:base + 6]
            mv = stats[:, base + 6:base + 8]
            rr = stats[:, base + 8:base + 9]
            b.op("v", lambda v, bn6=bn6, acc=acc: v.bn_stats(bn6, acc[:, :]),
                 reads=(acck,), writes=(f"bn:{tag}{nt}",))
            b.op("v", lambda v, mv=mv, bn6=bn6: v.bn_aggr(mv, bn6),
                 reads=(f"bn:{tag}{nt}",), writes=(f"mv:{tag}{nt}",))
            b.op("s", lambda s, rr=rr, mv=mv: s.activation(
                rr, mv[:, 1:2], AF.Sqrt, bias=epsb[:, 0:1], scale=F1),
                 reads=(f"mv:{tag}{nt}", "epsb"), writes=(f"r2:{tag}{nt}",))
            b.op("v", lambda v, rr=rr: v.reciprocal(rr, rr),
                 reads=(f"r2:{tag}{nt}",), writes=(f"r2:{tag}{nt}",))
            cqn = aB[:, cqnoff + nt * 512:cqnoff + (nt + 1) * 512]
            b.op("v", lambda v, cqn=cqn, acc=acc, mv=mv, rr=rr: v.tensor_scalar(
                cqn, acc[:, :], mv[:, 0:1], rr, ALU.subtract, ALU.mult),
                 reads=(acck, f"mv:{tag}{nt}", f"r2:{tag}{nt}"),
                 writes=(f"{tag}n{nt}", acck))
            if limit <= 0.8 and proj == "q" and nt == 0:
                b.dbg("stat_chk", [128, 64], F32, stats[:, :],
                      reads=(f"mv:{tag}0",))
                b.dbg("cqn0", [128, 512], BF16, cqn, reads=(f"{tag}n{nt}",))
                return _cut()
        fmoff = CQF if tag == "cq" else CQF + 4096
        for q2 in range(4):
            ph = q2 % 2
            pts = PT[:, ph * 1024:(ph + 1) * 1024]
            def tfn(t, q2=q2, pts=pts, cqnoff=cqnoff):
                instr = None
                for nt2 in range(NT):
                    instr = t.transpose(
                        pts[:, nt2 * 128:(nt2 + 1) * 128],
                        aB[:, cqnoff + nt2 * 512 + q2 * 128:cqnoff + nt2 * 512 + (q2 + 1) * 128],
                        idb[:, :])
                return instr
            b.op("t", tfn,
                 reads=tuple(f"{tag}n{nt2}" for nt2 in range(NT)) + ("idb",),
                 writes=(f"PTh{ph}",))
            dst = aB[:, fmoff + q2 * 1024:fmoff + (q2 + 1) * 1024]
            gb = gbuf[:, q2:q2 + 1]
            bb2 = bbuf[:, q2:q2 + 1]
            gkey = "qgb" if tag == "cq" else "kgb"
            bkey = "qbb" if tag == "cq" else "kbb"
            b.op("v", lambda v, dst=dst, pts=pts, gb=gb, bb2=bb2: v.tensor_scalar(
                dst, pts, gb, bb2, ALU.mult, ALU.add),
                 reads=(f"PTh{ph}", gkey, bkey), writes=(f"{tag}fm{q2}",))
    b.dbg("cq_fm", [128, 4096], BF16, aB[:, CQF:CQF + 4096],
          reads=tuple(f"cqfm{q2}" for q2 in range(4)))
    b.dbg("ckv_fm", [128, 4096], BF16, aB[:, CQF + 4096:CQF + 8192],
          reads=tuple(f"ckvfm{q2}" for q2 in range(4)))
    if limit <= 1:
        return _cut()

    b.dma("g", aB[:, WUQo:WUQo + 8192], wuq[:, :],
          writes=("wuqb",) + tuple(f"htm{nt}" for nt in range(NT)))
    b.dma("g", aB[:, WUKVo:WUKVo + 16384], wukv[:, :],
          writes=("wukvb", "wdqb", "wdkvb"))

    cqfm_reads = tuple(f"cqfm{q2}" for q2 in range(4))
    ckvfm_reads = tuple(f"ckvfm{q2}" for q2 in range(4))

    stg_state = {"i": 0}

    def stage_out(psum_ap, psum_key, scale, dst_ap, dkey):
        sl = stg_state["i"] % 4
        stg_state["i"] += 1
        st = aB[:, STG + sl * 512:STG + (sl + 1) * 512]
        skey = f"stg{sl}"
        if scale is None:
            b.op("v", lambda v, st=st, p=psum_ap: v.tensor_copy(st, p),
                 reads=(psum_key, skey), writes=(skey, psum_key))
        else:
            b.op("v", lambda v, st=st, p=psum_ap: v.tensor_scalar(
                st, p, scale, None, ALU.mult),
                 reads=(psum_key, skey), writes=(skey, psum_key))
        b.dma("y", dst_ap, st, reads=(skey,), writes=(dkey,))

    for oc in range(CCD):
        for ng in range(2):
            acc = P1 if (oc * 2 + ng) % 2 == 0 else P2
            acck = "P1" if (oc * 2 + ng) % 2 == 0 else "P2"
            pairs = []
            for kc in range(4):
                lh = aB[:, WUKVo + kc * 4096 + oc * 128:WUKVo + kc * 4096 + (oc + 1) * 128]
                rh = aB[:, CQF + 4096 + kc * 1024 + ng * 512:CQF + 4096 + kc * 1024 + (ng + 1) * 512]
                pairs.append((lh, rh))
            b.mmgroup(pairs, acc[:, :], reads=ckvfm_reads + ("wukvb",), writes=(acck,))
            d = ng * 4 + oc // 4
            off = (oc % 4) * 128 * 512
            dst = cc_in[d, off:off + 128 * 512].rearrange("(p n) -> p n", p=128)
            stage_out(acc[:, :], acck, None, dst, f"ccinK{d}:{oc % 4}")
    for nt in range(NT):
        for og in range(4):
            acc = P1 if (nt * 4 + og) % 2 == 0 else P2
            acck = "P1" if (nt * 4 + og) % 2 == 0 else "P2"
            pairs = []
            for kc in range(4):
                lh = aB[:, CQF + 4096 + kc * 1024 + nt * 128:CQF + 4096 + kc * 1024 + nt * 128 + 128]
                rh = aB[:, WUKVo + kc * 4096 + 2048 + og * 512:WUKVo + kc * 4096 + 2048 + (og + 1) * 512]
                pairs.append((lh, rh))
            b.mmgroup(pairs, acc[:, :], reads=ckvfm_reads + ("wukvb",), writes=(acck,))
            ci = nt // 2
            d = (ci // 2) * 4 + og
            roff = (ci % 2) * 256 + (nt % 2) * 128
            off = 512 * 512 + roff * 512
            dst = cc_in[d, off:off + 128 * 512].rearrange("(p n) -> p n", p=128)
            stage_out(acc[:, :], acck, None, dst, f"ccinV{d}:{roff // 128}")
    for oc in range(CCD):
        for ng in range(2):
            acc = P1 if (oc * 2 + ng) % 2 == 0 else P2
            acck = "P1" if (oc * 2 + ng) % 2 == 0 else "P2"
            pairs = []
            for kc in range(4):
                lh = aB[:, WUQo + kc * 2048 + oc * 128:WUQo + kc * 2048 + (oc + 1) * 128]
                rh = aB[:, CQF + kc * 1024 + ng * 512:CQF + kc * 1024 + (ng + 1) * 512]
                pairs.append((lh, rh))
            b.mmgroup(pairs, acc[:, :], reads=cqfm_reads + ("wuqb",), writes=(acck,))
            d = ng * 4 + oc // 4
            off = 2 * 512 * 512 + (oc % 4) * 128 * 512
            dst = cc_in[d, off:off + 128 * 512].rearrange("(p n) -> p n", p=128)
            stage_out(acc[:, :], acck, ISC, dst, f"ccinQ{d}:{oc % 4}")

    ccin_keys = tuple(f"ccinK{d}:{i}" for d in range(8) for i in range(4)) + \
        tuple(f"ccinV{d}:{i}" for d in range(8) for i in range(4)) + \
        tuple(f"ccinQ{d}:{i}" for d in range(8) for i in range(4))
    b.pre("g", ccin_keys, ("ccout",))
    cc_instr = nc.gpsimd.collective_compute(
        "AllToAll", ALU.bypass, replica_groups=[list(range(NCORE))],
        ins=[cc_in[:, :]], outs=[cc_out[:, :]])
    cc_instr.then_inc(b.sem["cc"], 1)
    b.cnt["cc"] += 1
    r = b._r("ccout"); r["w"] = ("cc", b.cnt["cc"]); r["r"] = {}
    for k in ccin_keys:
        b._r(k)["r"]["cc"] = b.cnt["cc"]

    for j in range(NCORE):
        blk = cc_out[j, :]
        for which, base, tag2 in (("K", 0, "ksb"), ("Q", 2 * 512 * 512, "qsb")):
            for hc in range(4):
                for half in range(2):
                    src = blk[base + hc * 128 * 512 + half * 256:
                              base + (hc * 128 + 127) * 512 + half * 256 + 256]
                    src2 = blk[base:base + 512 * 512].rearrange(
                        "(d t) -> d t", t=512)[hc * 128:(hc + 1) * 128,
                                               half * 256:(half + 1) * 256]
                    p = (j if half == 0 else 15 - j)
                    dsto = (KSB if which == "K" else QSB) + hc * 4096 + p * 256
                    b.dma("s", aB[:, dsto:dsto + 256], src2,
                          reads=("ccout",), writes=(f"{tag2}{hc}:{p}",))
        for half in range(2):
            for sub in range(2):
                src2 = blk[512 * 512:2 * 512 * 512].rearrange(
                    "(t d) -> t d", d=512)[half * 256 + sub * 128:
                                           half * 256 + (sub + 1) * 128, :]
                p = (j if half == 0 else 15 - j)
                vt = p * 2 + sub
                b.dma("s", aB[:, VSB + vt * 512:VSB + (vt + 1) * 512], src2,
                      reads=("ccout",), writes=(f"vsb{vt}",))

    if limit <= 2:
        b.dbg("ksb", [128, 16384], BF16, aB[:, KSB:KSB + 16384],
              reads=tuple(f"ksb{hc}:{p}" for hc in range(4) for p in range(16)))
        b.dbg("qsb", [128, 16384], BF16, aB[:, QSB:QSB + 16384],
              reads=tuple(f"qsb{hc}:{p}" for hc in range(4) for p in range(16)))
        b.dbg("vsb", [128, 16384], BF16, aB[:, VSB:VSB + 16384],
              reads=tuple(f"vsb{vt}" for vt in range(32)))
        return _cut()

    for qc in range(16):
        nkb = ceil_div(qc + 1, 2)
        passes = [(0, min(4, nkb))]
        if nkb > 4:
            passes.append((4, nkb))
        for qt in range(2):
            qcol = qc * 256 + qt * 128
            for hh in range(4):
                qread = (f"qsb{hh}:{p}" for p in range(qc + 1))
                lh = aB[:, QSB + hh * 4096 + qcol:QSB + hh * 4096 + qcol + 128]
                npass = len(passes)
                for pi, (k0, k1) in enumerate(passes):
                    nk = k1 - k0
                    kreads = tuple(f"ksb{hh}:{p}" for p in range(2 * k0, 2 * k1))
                    def sfn(t, lh=lh, k0=k0, k1=k1, hh=hh):
                        instr = None
                        for kk in range(k1 - k0):
                            rh = aB[:, KSB + hh * 4096 + (k0 + kk) * 512:
                                    KSB + hh * 4096 + (k0 + kk + 1) * 512]
                            instr = t.matmul(P0[:, kk * 512:(kk + 1) * 512], lh, rh,
                                             start=True, stop=True)
                        return instr
                    p0keys = tuple(f"P0:{i}" for i in range(nk))
                    b.op("t", sfn, reads=kreads + (f"qsb{hh}:{qc}",),
                         writes=p0keys)
                    if pi == npass - 1:
                        doff = (0 if qc % 2 == 0 else 256) + qt * 128
                        b.op("v", lambda v, nk=nk, doff=doff: v.tensor_tensor(
                            P0[:, (nk - 1) * 512:nk * 512],
                            P0[:, (nk - 1) * 512:nk * 512],
                            msk[:, 384 - doff:896 - doff], ALU.add),
                             reads=(f"P0:{nk-1}", "msk"), writes=(f"P0:{nk-1}",))
                    nm = lbuf[:, pi:pi + 1]
                    b.op("v", lambda v, nm=nm, nk=nk: v.reduce_max(
                        nm, P0[:, :nk * 512], axis=AX.X, negate=True),
                         reads=p0keys, writes=(f"nm{pi}",))
                    ll = lbuf[:, 4 + pi:5 + pi]
                    b.op("s", lambda s, nm=nm, ll=ll, nk=nk: s.activation(
                        ptl[:, 0:nk * 512], P0[:, :nk * 512], AF.Exp,
                        bias=nm, scale=F1, accum_out=ll),
                         reads=p0keys + (f"nm{pi}",), writes=("ptl", f"l{pi}") + p0keys)
                    nkt = nk * 4
                    for g0 in range(0, nkt, 8):
                        g1 = min(g0 + 8, nkt)
                        ph = (g0 // 8) % 2
                        pts = PT[:, ph * 1024:(ph + 1) * 1024]
                        def tfn(t, g0=g0, g1=g1, pts=pts):
                            instr = None
                            for kt in range(g0, g1):
                                instr = t.transpose(
                                    pts[:, (kt - g0) * 128:(kt - g0 + 1) * 128],
                                    ptl[:, kt * 128:(kt + 1) * 128],
                                    idb[:, :])
                            return instr
                        b.op("t", tfn, reads=("ptl", "idb"), writes=(f"PTh{ph}",))
                        b.op("v", lambda v, g0=g0, g1=g1, pts=pts: v.tensor_copy(
                            aB[:, PTT + g0 * 128:PTT + g1 * 128],
                            pts[:, :(g1 - g0) * 128]),
                             reads=(f"PTh{ph}",), writes=("ptt",))
                    vreads = tuple(f"vsb{vt}" for vt in range(4 * k0, 4 * k1))
                    def pvfn(t, k0=k0, k1=k1, hh=hh, nkt=nkt):
                        instr = None
                        n = nkt
                        for kt in range(n):
                            vt = k0 * 4 + kt
                            lh2 = aB[:, PTT + kt * 128:PTT + (kt + 1) * 128]
                            rh2 = aB[:, VSB + vt * 512 + hh * 128:VSB + vt * 512 + (hh + 1) * 128]
                            instr = t.matmul(P1[:, 0:128], lh2, rh2,
                                             start=(kt == 0), stop=(kt == n - 1))
                        return instr
                    b.op("t", pvfn, reads=("ptt",) + vreads, writes=("P1",))
                    if npass == 2 and pi == 0:
                        o0 = o0f[:, hh * 256 + qt * 128:hh * 256 + (qt + 1) * 128]
                        b.op("v", lambda v, o0=o0: v.tensor_copy(o0, P1[:, 0:128]),
                             reads=("P1",), writes=("o0f", "P1"))
                otile = qc * 2 + qt
                odst = aB[:, OSB + otile * 512 + hh * 128:OSB + otile * 512 + (hh + 1) * 128]
                if npass == 1:
                    rl = lbuf[:, 8:9]
                    b.op("v", lambda v, rl=rl: v.reciprocal(rl, lbuf[:, 4:5]),
                         reads=("l0",), writes=("rl",))
                    b.op("v", lambda v, odst=odst, rl=rl: v.tensor_scalar(
                        odst, P1[:, 0:128], rl, None, ALU.mult),
                         reads=("P1", "rl"), writes=(f"osb{otile}:{hh}", "P1"))
                else:
                    nmf = lbuf[:, 9:10]; a0 = lbuf[:, 10:11]; a1 = lbuf[:, 11:12]
                    lf = lbuf[:, 12:13]; rl = lbuf[:, 8:9]
                    t2c = lbuf[:, 13:14]
                    b.op("v", lambda v, nmf=nmf: v.tensor_tensor(
                        nmf, lbuf[:, 0:1], lbuf[:, 1:2], ALU.min),
                         reads=("nm0", "nm1"), writes=("nmf",))
                    b.op("s", lambda s, a0=a0, nmf=nmf: s.activation(
                        a0, lbuf[:, 0:1], AF.Exp, bias=nmf, scale=FNEG1),
                         reads=("nm0", "nmf"), writes=("a0",))
                    b.op("s", lambda s, a1=a1, nmf=nmf: s.activation(
                        a1, lbuf[:, 1:2], AF.Exp, bias=nmf, scale=FNEG1),
                         reads=("nm1", "nmf"), writes=("a1",))
                    b.op("v", lambda v, t2c=t2c, a1=a1: v.tensor_tensor(
                        t2c, lbuf[:, 5:6], a1, ALU.mult),
                         reads=("l1", "a1"), writes=("t2c",))
                    b.op("v", lambda v, lf=lf, a0=a0, t2c=t2c: v.scalar_tensor_tensor(
                        lf, lbuf[:, 4:5], a0, t2c, ALU.mult, ALU.add),
                         reads=("l0", "a0", "t2c"), writes=("lf",))
                    b.op("v", lambda v, rl=rl, lf=lf: v.reciprocal(rl, lf),
                         reads=("lf",), writes=("rl",))
                    o0 = o0f[:, hh * 256 + qt * 128:hh * 256 + (qt + 1) * 128]
                    b.op("v", lambda v, a1=a1: v.tensor_scalar(
                        P1[:, 128:256], P1[:, 0:128], a1, None, ALU.mult),
                         reads=("P1", "a1"), writes=("P1",))
                    b.op("v", lambda v, o0=o0, a0=a0: v.scalar_tensor_tensor(
                        P1[:, 256:384], o0, a0, P1[:, 128:256], ALU.mult, ALU.add),
                         reads=("o0f", "a0", "P1"), writes=("P1",))
                    b.op("v", lambda v, odst=odst, rl=rl: v.tensor_scalar(
                        odst, P1[:, 256:384], rl, None, ALU.mult),
                         reads=("P1", "rl"), writes=(f"osb{otile}:{hh}", "P1"))

    if limit <= 3:
        b.dbg("osb", [128, 16384], BF16, aB[:, OSB:OSB + 16384],
              reads=tuple(f"osb{t2}:{hh}" for t2 in range(32) for hh in range(4)))
        return _cut()

    for tile in range(32):
        p = tile // 2
        sub = tile % 2
        d = p if p < 8 else 15 - p
        roff = (0 if p < 8 else 256) + sub * 128
        dst = cc2_in[d, roff * 512:(roff + 128) * 512].rearrange("(p n) -> p n", p=128)
        b.dma("y", dst, aB[:, OSB + tile * 512:OSB + (tile + 1) * 512],
              reads=tuple(f"osb{tile}:{hh}" for hh in range(4)),
              writes=(f"cc2in{tile}",))
    b.pre("g", tuple(f"cc2in{t2}" for t2 in range(32)), ("cc2out",))
    cc2 = nc.gpsimd.collective_compute(
        "AllToAll", ALU.bypass, replica_groups=[list(range(NCORE))],
        ins=[cc2_in[:, :]], outs=[cc2_out[:, :]])
    cc2.then_inc(b.sem["cc"], 1)
    b.cnt["cc"] += 1
    r = b._r("cc2out"); r["w"] = ("cc", b.cnt["cc"]); r["r"] = {}
    for t2 in range(32):
        b._r(f"cc2in{t2}")["r"]["cc"] = b.cnt["cc"]

    for nt in range(NT):
        ci = nt // 2
        bsel = ci // 2
        roff = (ci % 2) * 256 + (nt % 2) * 128
        for hg in range(4):
            s2 = bsel * 4 + hg
            src = cc2_out[s2, roff * 512:(roff + 128) * 512].rearrange(
                "(p n) -> p n", p=128)
            b.dma("y", aB[:, OTM + nt * 2048 + hg * 512:OTM + nt * 2048 + (hg + 1) * 512],
                  src, reads=("cc2out",), writes=(f"otm{nt}",))
    b.dbg("o_tm", [128, 16384], BF16, aB[:, OTM:OTM + 16384],
          reads=tuple(f"otm{nt}" for nt in range(NT)))
    for jc in range(CCD):
        ph = jc % 2
        pts = PT[:, ph * 1024:(ph + 1) * 1024]
        def tfn(t, jc=jc, pts=pts):
            instr = None
            for nt2 in range(NT):
                instr = t.transpose(
                    pts[:, nt2 * 128:(nt2 + 1) * 128],
                    aB[:, OTM + nt2 * 2048 + jc * 128:OTM + nt2 * 2048 + (jc + 1) * 128],
                    idb[:, :])
            return instr
        b.op("t", tfn, reads=tuple(f"otm{nt2}" for nt2 in range(NT)) + ("idb",),
             writes=(f"PTh{ph}",))
        b.op("v", lambda v, jc=jc, pts=pts: v.tensor_copy(
            aB[:, OFM + jc * 1024:OFM + (jc + 1) * 1024], pts),
             reads=(f"PTh{ph}",), writes=(f"ofm{jc}",))
    for jc in range(CCD):
        b.dma("g", aB[:, WOB + jc * 2048:WOB + (jc + 1) * 2048],
              wo[:, jc * 2048:(jc + 1) * 2048],
              reads=(), writes=(f"wob{jc}",))
    ofm_reads = tuple(f"ofm{jc}" for jc in range(CCD))
    for nt in range(NT):
        xr = aF[:, XR + (nt % 2) * 2048:XR + (nt % 2 + 1) * 2048]
        b.dma("y", xr, x_own[nt * 128:(nt + 1) * 128, :],
              writes=(f"xrl{nt}",) + ((f"xrl{nt - 2}",) if nt >= 2 else ()))
        x2t = aF[:, X2T + (nt % 2) * 2048:X2T + (nt % 2 + 1) * 2048]
        for ig in range(4):
            acc = P1 if ig % 2 == 0 else P2
            acck = "P1" if ig % 2 == 0 else "P2"
            pairs = []
            for jc in range(CCD):
                lh = aB[:, OFM + jc * 1024 + nt * 128:OFM + jc * 1024 + nt * 128 + 128]
                rh = aB[:, WOB + jc * 2048 + ig * 512:WOB + jc * 2048 + (ig + 1) * 512]
                pairs.append((lh, rh))
            b.mmgroup(pairs, acc[:, :],
                      reads=ofm_reads + tuple(f"wob{jc}" for jc in range(CCD)),
                      writes=(acck,))
            b.op("v", lambda v, x2t=x2t, acc=acc, ig=ig, xr=xr: v.tensor_tensor(
                x2t[:, ig * 512:(ig + 1) * 512], acc[:, :],
                xr[:, ig * 512:(ig + 1) * 512], ALU.add),
                 reads=(acck, f"xrl{nt}"), writes=(f"x2t{nt % 2}:{ig}", acck))
        x2keys = tuple(f"x2t{nt % 2}:{ig}" for ig in range(4))
        b.dma("y", x2h[nt * 128:(nt + 1) * 128, :], x2t,
              reads=x2keys, writes=(f"x2h{nt}",))
        ssl = stats[:, nt:nt + 1]; sql = stats[:, 8 + nt:8 + nt + 1]
        nmu = stats[:, 16 + nt:16 + nt + 1]; rr = stats[:, 24 + nt:24 + nt + 1]
        b.op("v", lambda v, ssl=ssl, x2t=x2t: v.reduce_sum(ssl, x2t, axis=AX.X),
             reads=x2keys, writes=(f"s4:{nt}",))
        b.op("s", lambda s, sql=sql, x2t=x2t: s.activation(
            aF[:, SCR:SCR + 2048], x2t, AF.Square, accum_out=sql),
             reads=x2keys, writes=(f"sq4:{nt}", "scr"))
        b.op("v", lambda v, ssl=ssl, nmu=nmu: v.tensor_scalar(
            nmu, ssl, (-1.0 / D), None, ALU.mult),
             reads=(f"s4:{nt}",), writes=(f"nmu4:{nt}",))
        b.op("v", lambda v, sql=sql, rr=rr: v.tensor_scalar(
            rr, sql, (1.0 / D), None, ALU.mult),
             reads=(f"sq4:{nt}",), writes=(f"r4:{nt}",))
        b.op("v", lambda v, nmu=nmu, rr=rr: v.scalar_tensor_tensor(
            rr, nmu, nmu, rr, ALU.mult, ALU.subtract),
             reads=(f"nmu4:{nt}", f"r4:{nt}"), writes=(f"r4:{nt}",))
        b.op("s", lambda s, rr=rr: s.activation(
            rr, rr, AF.Sqrt, bias=epsb[:, 0:1], scale=FNEG1),
             reads=(f"r4:{nt}", "epsb"), writes=(f"r4:{nt}",))
        b.op("v", lambda v, rr=rr: v.reciprocal(rr, rr),
             reads=(f"r4:{nt}",), writes=(f"r4:{nt}",))
        ht = aB[:, H2TM + nt * 2048:H2TM + (nt + 1) * 2048]
        b.op("v", lambda v, ht=ht, x2t=x2t, nmu=nmu, rr=rr: v.tensor_scalar(
            ht, x2t, nmu, rr, ALU.add, ALU.mult),
             reads=x2keys + (f"nmu4:{nt}", f"r4:{nt}"),
             writes=(f"h2tm{nt}", f"x2t{nt % 2}:0", f"x2t{nt % 2}:1",
                     f"x2t{nt % 2}:2", f"x2t{nt % 2}:3"))
    if "x2" in b.dbg_names:
        t_x2 = nc.dram_tensor("dbg_x2", [T, D], F32, kind="ExternalOutput").ap()
        b.dbg_tensors["x2"] = t_x2
        b.dma("y", t_x2[:, :], x2h[:, :],
              reads=tuple(f"x2h{nt}" for nt in range(NT)), writes=("dbg_x2",))

    for cc in range(CCD):
        ph = cc % 2
        pts = PT[:, ph * 1024:(ph + 1) * 1024]
        def tfn(t, cc=cc, pts=pts):
            instr = None
            for nt2 in range(NT):
                instr = t.transpose(
                    pts[:, nt2 * 128:(nt2 + 1) * 128],
                    aB[:, H2TM + nt2 * 2048 + cc * 128:H2TM + nt2 * 2048 + (cc + 1) * 128],
                    idb[:, :])
            return instr
        b.op("t", tfn, reads=tuple(f"h2tm{nt2}" for nt2 in range(NT)) + ("idb",),
             writes=(f"PTh{ph}",))
        b.op("v", lambda v, cc=cc, pts=pts: v.tensor_copy(
            aB[:, H2FM + cc * 1024:H2FM + (cc + 1) * 1024], pts),
             reads=(f"PTh{ph}",), writes=(f"h2fm{cc}",))
    h2fm_reads = tuple(f"h2fm{cc}" for cc in range(CCD))
    b.dbg("h2_fm", [128, 16384], BF16, aB[:, H2FM:H2FM + 16384], reads=h2fm_reads)
    if limit <= 4:
        return _cut()

    for nt in range(NT):
        pairs = []
        for kc in range(CCD):
            lh = aB[:, H2FM + kc * 1024 + nt * 128:H2FM + kc * 1024 + nt * 128 + 128]
            rh = gsb[:, kc * 8:(kc + 1) * 8]
            pairs.append((lh, rh))
        b.mmgroup(pairs, P0[:, 0:8], reads=h2fm_reads + ("gsb",), writes=("P0:0",))
        prb = gprb[:, nt * 8:nt * 8 + 8]
        wrk = aF[:, PRB + nt * 64 + 8:PRB + nt * 64 + 16]
        cmb = aF[:, PRB + nt * 64 + 16:PRB + nt * 64 + 24]
        nm = lbuf[:, 16:17]; sm = lbuf[:, 17:18]; rs = lbuf[:, 18:19]
        m1 = lbuf[:, 19:20]; m2 = lbuf[:, 20:21]
        b.op("v", lambda v, nm=nm: v.reduce_max(nm, P0[:, 0:8], axis=AX.X, negate=True),
             reads=("P0:0",), writes=("gnm",))
        b.op("s", lambda s, prb=prb, nm=nm, sm=sm: s.activation(
            prb, P0[:, 0:8], AF.Exp, bias=nm, scale=F1, accum_out=sm),
             reads=("P0:0", "gnm"), writes=(f"gprb{nt}", "gsm", "P0:0"))
        b.op("v", lambda v, rs=rs, sm=sm: v.reciprocal(rs, sm),
             reads=("gsm",), writes=("grs",))
        b.op("v", lambda v, prb=prb, rs=rs: v.tensor_scalar(
            prb, prb, rs, None, ALU.mult),
             reads=(f"gprb{nt}", "grs"), writes=(f"gprb{nt}",))
        b.op("v", lambda v, m1=m1, prb=prb: v.reduce_max(m1, prb, axis=AX.X),
             reads=(f"gprb{nt}",), writes=("gm1",))
        b.op("v", lambda v, wrk=wrk, prb=prb, m1=m1: v.tensor_scalar(
            wrk, prb, m1, None, ALU.is_ge),
             reads=(f"gprb{nt}", "gm1"), writes=(f"gwrk{nt}",))
        b.op("v", lambda v, wrk=wrk, prb=prb: v.scalar_tensor_tensor(
            wrk, wrk, (-1e30), prb, ALU.mult, ALU.add),
             reads=(f"gwrk{nt}", f"gprb{nt}"), writes=(f"gwrk{nt}",))
        b.op("v", lambda v, m2=m2, wrk=wrk: v.reduce_max(m2, wrk, axis=AX.X),
             reads=(f"gwrk{nt}",), writes=("gm2",))
        b.op("v", lambda v, cmb=cmb, prb=prb, m2=m2: v.tensor_scalar(
            cmb, prb, m2, None, ALU.is_ge),
             reads=(f"gprb{nt}", "gm2"), writes=(f"gcmb{nt}",))
        b.op("v", lambda v, cmb=cmb, prb=prb: v.tensor_tensor(
            cmb, cmb, prb, ALU.mult),
             reads=(f"gcmb{nt}", f"gprb{nt}"), writes=(f"gcmb{nt}",))
        b.op("t", lambda t, cmb=cmb, nt=nt: t.transpose(
            P0[0:8, 1024 + nt * 128:1024 + (nt + 1) * 128], cmb, idf[:, :]),
             reads=(f"gcmb{nt}", "idf"), writes=("P0comb",))
    b.op("v", lambda v: v.tensor_copy(combr[:, :], P0[0:8, 1024:2048]),
         reads=("P0comb",), writes=("combr",))
    b.dbg("comb_rows", [8, 1024], F32, combr[:, :], reads=("combr",))
    cflat1 = aF[0:1, 14336:15360]
    if limit <= 5:
        return _cut()

    def expert_pass(eidx, n_ic, w1src, w2src, w3src, use_comb, tagp):
        first = (eidx == 0)
        if use_comb:
            b.dma("y", cflat1[0:1, :], combr[eidx:eidx + 1, :],
                  reads=("combr",), writes=("cflat",))
            def cbfn(t):
                instr = None
                for ng in range(2):
                    instr = t.matmul(P0[:, 1024 + ng * 512:1024 + (ng + 1) * 512],
                                     onef[:, :],
                                     cflat1[0:1, ng * 512:(ng + 1) * 512],
                                     start=True, stop=True)
                return instr
            b.op("t", cbfn, reads=("onef", "cflat"), writes=("P0:2", "P0:3"))
            b.op("v", lambda v: v.tensor_copy(aF[:, CBC:CBC + 1024], P0[:, 1024:2048]),
                 reads=("P0:2", "P0:3"), writes=("cbc", "P0:2", "P0:3"))
        for ic in range(n_ic):
            sl = ic % 2
            w1b = aB[:, W12 + sl * 2048:W12 + (sl + 1) * 2048]
            w2b = aB[:, W12 + 4096 + sl * 2048:W12 + 4096 + (sl + 1) * 2048]
            cl1 = (f"h2tm{sl}",) if (first and ic < 2) else ()
            cl2 = (f"h2tm{2 + sl}",) if (first and ic < 2) else ()
            b.dma("g", w1b, w1src[ic, :, :], writes=(f"w1b{sl}",) + cl1)
            b.dma("g", w2b, w2src[ic, :, :], writes=(f"w2b{sl}",) + cl2)
            if ic == 1:
                for jc3 in range(n_ic):
                    cl3 = (tuple(f"wob{jj}" for jj in range(5, 16))
                           if (first and jc3 == 0) else ())
                    b.dma("g", aB[:, W3B + jc3 * 2048:W3B + (jc3 + 1) * 2048],
                          w3src[jc3, :, :], writes=(f"w3b{jc3}",) + cl3)
            for ng in range(2):
                accs = [(P1, "P1"), (P2, "P2")]
                (a1, k1), (a2, k2) = accs
                pairs1 = []
                pairs2 = []
                for kc in range(CCD):
                    lh1 = w1b[:, kc * 128:(kc + 1) * 128]
                    lh2 = w2b[:, kc * 128:(kc + 1) * 128]
                    rh = aB[:, H2FM + kc * 1024 + ng * 512:H2FM + kc * 1024 + (ng + 1) * 512]
                    pairs1.append((lh1, rh))
                    pairs2.append((lh2, rh))
                b.mmgroup(pairs1, a1[:, :], reads=h2fm_reads + (f"w1b{sl}",), writes=(k1,))
                b.mmgroup(pairs2, a2[:, :], reads=h2fm_reads + (f"w2b{sl}",), writes=(k2,))
                sil = aF[:, SIL + ng * 512:SIL + (ng + 1) * 512]
                b.op("s", lambda s, sil=sil, a1=a1: s.activation(sil, a1[:, :], AF.Silu),
                     reads=(k1,), writes=(f"sil{ng}", k1))
                t2 = aF[:, T2R + ng * 512:T2R + (ng + 1) * 512]
                b.op("v", lambda v, t2=t2, a2=a2, sil=sil: v.tensor_tensor(
                    t2, a2[:, :], sil, ALU.mult),
                     reads=(k2, f"sil{ng}"), writes=(f"t2r{ng}", k2))
                hdst = aB[:, HEB + ic * 1024 + ng * 512:HEB + ic * 1024 + (ng + 1) * 512]
                if use_comb:
                    b.op("g", lambda g, hdst=hdst, t2=t2, ng=ng: g.tensor_tensor(
                        hdst, t2, aF[:, CBC + ng * 512:CBC + (ng + 1) * 512], ALU.mult),
                         reads=(f"t2r{ng}", "cbc"), writes=(f"heb{ic}",))
                else:
                    b.op("g", lambda g, hdst=hdst, t2=t2: g.tensor_copy(hdst, t2),
                         reads=(f"t2r{ng}",), writes=(f"heb{ic}",))
        heb_reads = tuple(f"heb{ic}" for ic in range(n_ic))
        w3_reads = tuple(f"w3b{ic}" for ic in range(n_ic))
        for nt in range(NT):
            for cg in range(4):
                pslot = (nt * 4 + cg) % 4
                acc = P0[:, pslot * 512:(pslot + 1) * 512]
                acck = f"P0:{pslot}"
                pairs = []
                for ic in range(n_ic):
                    lh = aB[:, HEB + ic * 1024 + nt * 128:HEB + ic * 1024 + nt * 128 + 128]
                    rh = aB[:, W3B + ic * 2048 + cg * 512:W3B + ic * 2048 + (cg + 1) * 512]
                    pairs.append((lh, rh))
                b.mmgroup(pairs, acc, reads=heb_reads + w3_reads, writes=(acck,))
                ysl = (nt * 4 + cg) % 2
                yev = aF[:, YEV + ysl * 512:YEV + (ysl + 1) * 512]
                b.op("v", lambda v, yev=yev, acc=acc: v.tensor_copy(yev, acc),
                     reads=(acck,), writes=(f"yev{ysl}", acck))
                dst = yh[nt * 128:(nt + 1) * 128, cg * 512:(cg + 1) * 512]
                b.dma("g", dst, yev, reads=(f"yev{ysl}",),
                      writes=(f"yh{nt}:{cg}",),
                      accum_op=(ALU.add if eidx > 0 else ALU.bypass))

    for e in range(E):
        expert_pass(e, NIC,
                    w1t[e * NIC:(e + 1) * NIC, :, :],
                    w2t[e * NIC:(e + 1) * NIC, :, :],
                    w3t[e * NIC:(e + 1) * NIC, :, :],
                    use_comb=True, tagp=f"e{e}")
    expert_pass(100, NISC // 2, sw1t[0:11, :, :], sw2t[0:11, :, :], sw3t[0:11, :, :],
                use_comb=False, tagp="sA")
    expert_pass(101, NISC // 2, sw1t[11:22, :, :], sw2t[11:22, :, :], sw3t[11:22, :, :],
                use_comb=False, tagp="sB")

    FX = 0
    FY = 4096
    FO = 8192
    for nt in range(NT):
        sl = nt % 2
        fx = aF[:, FX + sl * 2048:FX + (sl + 1) * 2048]
        fy = aF[:, FY + sl * 2048:FY + (sl + 1) * 2048]
        fo = aF[:, FO + sl * 2048:FO + (sl + 1) * 2048]
        b.dma("y", fx, x2h[nt * 128:(nt + 1) * 128, :],
              reads=(f"x2h{nt}",), writes=(f"fx{sl}",))
        b.dma("y", fy, yh[nt * 128:(nt + 1) * 128, :],
              reads=tuple(f"yh{nt}:{cg}" for cg in range(4)), writes=(f"fy{sl}",))
        b.op("v", lambda v, fo=fo, fx=fx, fy=fy: v.tensor_tensor(
            fo, fx, fy, ALU.add),
             reads=(f"fx{sl}", f"fy{sl}"), writes=(f"fo{sl}", f"fx{sl}", f"fy{sl}"))
        b.dma("y", out[nt * 128:(nt + 1) * 128, :], fo,
              reads=(f"fo{sl}",), writes=(f"outd{nt}",))

    b.pre("y", tuple(f"outd{nt}" for nt in range(NT)), ())
    return b



def _relayout(inputs):
    g = {}
    f32 = np.float32
    g["wdq"] = np.ascontiguousarray(
        inputs["L0_attn_W_dq"].reshape(16, 128, 512).transpose(1, 0, 2).reshape(128, -1))
    g["wdkv"] = np.ascontiguousarray(
        inputs["L0_attn_W_dkv"].reshape(16, 128, 512).transpose(1, 0, 2).reshape(128, -1))
    g["wuq"] = np.ascontiguousarray(
        inputs["L0_attn_W_uq"].reshape(4, 128, 2048).transpose(1, 0, 2).reshape(128, -1))
    g["wukv"] = np.ascontiguousarray(
        inputs["L0_attn_W_ukv"].reshape(4, 128, 4096).transpose(1, 0, 2).reshape(128, -1))
    g["wo"] = np.ascontiguousarray(
        inputs["L0_attn_W_o"].T.reshape(16, 128, 2048).transpose(1, 0, 2).reshape(128, -1))
    g["qg"] = np.ascontiguousarray(inputs["L0_attn_q_ln_g"].reshape(4, 128).T)
    g["qb"] = np.ascontiguousarray(inputs["L0_attn_q_ln_b"].reshape(4, 128).T)
    g["kg"] = np.ascontiguousarray(inputs["L0_attn_kv_ln_g"].reshape(4, 128).T)
    g["kb"] = np.ascontiguousarray(inputs["L0_attn_kv_ln_b"].reshape(4, 128).T)
    g["gate"] = np.ascontiguousarray(
        inputs["L0_ffn_gate_w"].T.reshape(16, 128, 8).transpose(1, 0, 2).reshape(128, -1))

    def tile_w12(w):
        E_, I_, C_ = w.shape
        nic = I_ // 128
        wt = w.transpose(0, 2, 1)
        out = wt.reshape(E_, 16, 128, nic, 128).transpose(0, 3, 2, 1, 4)
        return np.ascontiguousarray(out.reshape(E_ * nic, 128, 16 * 128))

    def tile_w3(w):
        E_, C_, I_ = w.shape
        nic = I_ // 128
        wt = w.transpose(0, 2, 1)
        return np.ascontiguousarray(wt.reshape(E_ * nic, 128, C_))

    g["w1t"] = tile_w12(inputs["L0_ffn_w1"])
    g["w2t"] = tile_w12(inputs["L0_ffn_w2"])
    g["w3t"] = tile_w3(inputs["L0_ffn_w3"])
    g["sw1t"] = tile_w12(inputs["L0_ffn_sw1"][None])
    g["sw2t"] = tile_w12(inputs["L0_ffn_sw2"][None])
    g["sw3t"] = tile_w3(inputs["L0_ffn_sw3"][None])
    g["identb"] = np.eye(128, dtype=ml_dtypes.bfloat16)
    g["identf"] = np.eye(128, dtype=np.float32)
    i = np.arange(128)[:, None]
    c = np.arange(896)[None, :]
    g["masks"] = np.ascontiguousarray(
        np.where(c - 384 <= i, 0.0, NEGBIG).astype(np.float32))
    return g


_BUILD_CACHE = {}


def kernel(**inputs):
    x = inputs["L0_attn_x"]
    g = _relayout(inputs)
    dbg = tuple(sorted(kernel.debug)) if getattr(kernel, "debug", None) else ()
    limit = getattr(kernel, "limit", 99)
    key = (dbg, limit)
    if key not in _BUILD_CACHE:
        with allow_ici_all_to_all():
            _BUILD_CACHE[key] = build_program(dbg, limit)
    b = _BUILD_CACHE[key]
    in_maps = []
    for r in range(NCORE):
        chunks = [x[0, 256 * r:256 * (r + 1)], x[0, 256 * (15 - r):256 * (16 - r)],
                  x[1, 256 * r:256 * (r + 1)], x[1, 256 * (15 - r):256 * (16 - r)]]
        m = dict(g)
        m["x_own"] = np.ascontiguousarray(np.concatenate(chunks, axis=0))
        in_maps.append(m)
    res = run_bass_kernel_spmd(b.nc, in_maps, core_ids=list(range(NCORE)))
    outp = np.zeros((2, 4096, 2048), dtype=np.float32)
    for r in range(NCORE):
        o = res.results[r]["out"]
        outp[0, 256 * r:256 * (r + 1)] = o[0:256]
        outp[0, 256 * (15 - r):256 * (16 - r)] = o[256:512]
        outp[1, 256 * r:256 * (r + 1)] = o[512:768]
        outp[1, 256 * (15 - r):256 * (16 - r)] = o[768:1024]
    kernel.last_results = res
    return outp


kernel.debug = ()


# version 08 (final)
# speedup vs baseline: 1.2231x; 1.1845x over previous
import math
from contextlib import ExitStack

import numpy as np
import ml_dtypes

import concourse.bass as bass
import concourse.mybir as mybir
from concourse.bass_utils import run_bass_kernel_spmd
from concourse.aot_env import allow_ici_all_to_all

F32 = mybir.dt.float32
BF16 = mybir.dt.bfloat16
AF = mybir.ActivationFunctionType
ALU = mybir.AluOpType
AX = mybir.AxisListType

D = 2048
QP = 512
NH = 16
DH = 128
BAT = 2
S = 4096
CHT = 256
T = 1024
NT = 8
CCD = 16
E = 8
I1 = 1408
NIC = 11
IS = 2816
NISC = 22
EPS = 1e-5
ISC = 1.0 / math.sqrt(128.0)
NEGBIG = -1e30
NCORE = 8
HG = 4
QBLK = 786432
OBLK = 262144

F1 = 1.0
FNEG1 = -1.0


class Builder:
    def __init__(self, dbg=()):
        self.nc = bass.Bass("TRN2", target_bir_lowering=False, debug=False)
        self.es = ExitStack()
        self.dbg_names = set(dbg)
        self.dbg_tensors = {}
        nc = self.nc
        self.eng = {
            "t": nc.tensor, "v": nc.vector, "s": nc.scalar,
            "g": nc.gpsimd, "y": nc.sync,
        }
        self.sem = {}
        for name in ("pt", "pv", "ps", "pg", "dy", "dg", "dc", "cc"):
            self.sem[name] = self.es.enter_context(nc.semaphore(name))
        self.cnt = {k: 0 for k in self.sem}
        self.eng_sem = {"t": "pt", "v": "pv", "s": "ps", "g": "pg"}
        self.q_sem = {"y": "dy", "g": "dg", "s": "dc"}
        self.reg = {}
        self.waited = {}

    def _r(self, key):
        if key not in self.reg:
            self.reg[key] = {"w": None, "r": {}}
        return self.reg[key]

    def _wait(self, ename, sem, val):
        if val <= 0:
            return
        assert val <= self.cnt[sem], (ename, sem, val, self.cnt[sem])
        k = (ename, sem)
        if self.waited.get(k, 0) >= val:
            return
        self.eng[ename].wait_ge(self.sem[sem], val)
        self.waited[k] = val

    def pre(self, ename, reads, writes):
        own = self.eng_sem.get(ename)
        for key in reads:
            r = self._r(key)
            if r["w"] is not None:
                sem, c = r["w"]
                if sem == own:
                    self._wait(ename, sem, c)
                else:
                    self._wait(ename, sem, c)
        for key in writes:
            r = self._r(key)
            if r["w"] is not None:
                sem, c = r["w"]
                self._wait(ename, sem, c)
            for sem, c in r["r"].items():
                self._wait(ename, sem, c)

    def post(self, ename, instr, reads, writes, dma=False):
        if dma:
            sem = self.q_sem[ename]
            instr.then_inc(self.sem[sem], 16)
            self.cnt[sem] += 16
            c = self.cnt[sem]
        else:
            sem = self.eng_sem[ename]
            instr.then_inc(self.sem[sem], 1)
            self.cnt[sem] += 1
            c = self.cnt[sem]
        for key in writes:
            r = self._r(key)
            r["w"] = (sem, c)
            r["r"] = {}
        for key in reads:
            r = self._r(key)
            r["r"][sem] = c

    def op(self, ename, fn, reads=(), writes=()):
        self.pre(ename, reads, writes)
        instr = fn(self.eng[ename])
        self.post(ename, instr, reads, writes)

    def dma(self, qname, out, in_, reads=(), writes=(), **kw):
        self.pre(qname, reads, writes)
        instr = self.eng[qname].dma_start(out, in_, **kw)
        self.post(qname, instr, reads, writes, dma=True)

    def dbg(self, name, shape, dtype, src_ap, reads):
        if name not in self.dbg_names:
            return
        t = self.nc.dram_tensor("dbg_" + name, list(shape), dtype,
                                kind="ExternalOutput").ap()
        self.dbg_tensors[name] = t
        self.dma("y", t[:, :] if len(shape) == 2 else t[:], src_ap,
                 reads=reads, writes=("dbg_" + name,))

    def mmgroup(self, pairs, out_ap, reads, writes):
        def fn(t):
            n = len(pairs)
            instr = None
            for i, (lh, rh) in enumerate(pairs):
                instr = t.matmul(out_ap, lh, rh, start=(i == 0), stop=(i == n - 1))
            return instr
        self.op("t", fn, reads=reads, writes=writes)


def ceil_div(a, b):
    return (a + b - 1) // b


def build_program(dbg=(), limit=99):
    b = Builder(dbg)
    nc = b.nc
    es = b.es

    def _cut():
        for k in list(b.reg):
            if k.startswith("dbg_"):
                b.pre("y", (k,), ())
        return b

    def din(name, shape, dtype=F32):
        return nc.dram_tensor(name, list(shape), dtype, kind="ExternalInput").ap()

    x_own = din("x_own", [T, D])
    wdq = din("wdq", [128, CCD * 512])
    wdkv = din("wdkv", [128, CCD * 512])
    wuq = din("wuq", [128, 4 * D])
    wukv = din("wukv", [128, 4 * 2 * D])
    wo = din("wo", [128, CCD * D])
    qg = din("qg", [128, 4]); qb = din("qb", [128, 4])
    kg = din("kg", [128, 4]); kb_ = din("kb", [128, 4])
    gate = din("gate", [128, CCD * 8])
    w1t = din("w1t", [E * NIC, 128, CCD * 128])
    w2t = din("w2t", [E * NIC, 128, CCD * 128])
    w3t = din("w3t", [E * NIC, 128, D])
    sw1t = din("sw1t", [NISC, 128, CCD * 128])
    sw2t = din("sw2t", [NISC, 128, CCD * 128])
    sw3t = din("sw3t", [NISC, 128, D])
    identb = din("identb", [128, 128], BF16)
    identf = din("identf", [128, 128], F32)
    masks = din("masks", [128, 896], F32)
    out = nc.dram_tensor("out", [T, D], F32, kind="ExternalOutput").ap()

    cc_in = nc.dram_tensor("cc_in", [NCORE, QBLK], BF16).ap()
    cc_out = nc.dram_tensor("cc_out", [NCORE, QBLK], BF16).ap()
    cc2_in = nc.dram_tensor("cc2_in", [NCORE, OBLK], BF16).ap()
    cc2_out = nc.dram_tensor("cc2_out", [NCORE, OBLK], BF16).ap()
    x2h = nc.dram_tensor("x2h", [T, D], F32).ap()
    yh = nc.dram_tensor("yh", [T, D], F32).ap()

    def sb(name, shape, dtype=F32):
        return es.enter_context(nc.sbuf_tensor(name, list(shape), dtype))

    ptl = sb("ptl", [128, 2048], BF16)
    scrA = sb("scrA", [128, 512], F32)
    gprb = sb("gprb", [128, 64], F32)

    aB = sb("aB", [128, 67584], BF16)
    aF = sb("aF", [128, 15360], F32)
    idb = sb("idb", [128, 128], BF16)
    idf = sb("idf", [128, 128], F32)
    msk = sb("msk", [128, 896], F32)
    qgb = sb("qgb", [128, 4]); qbb = sb("qbb", [128, 4])
    kgb = sb("kgb", [128, 4]); kbb = sb("kbb", [128, 4])
    gsb = sb("gsb", [128, CCD * 8], BF16)
    onef = sb("onef", [1, 128], F32)
    epsb = sb("epsb", [128, 1], F32)
    stats = sb("stats", [128, 64], F32)
    combr = aF[0:8, 10752:11776]
    lbuf = sb("lbuf", [128, 32], F32)
    o0f = aF[:, 0:1024]

    P0 = es.enter_context(nc.psum_tensor("P0", [128, 2048], F32))
    P1 = es.enter_context(nc.psum_tensor("P1", [128, 512], F32))
    P2 = es.enter_context(nc.psum_tensor("P2", [128, 512], F32))
    PT = es.enter_context(nc.psum_tensor("PT", [128, 2048], BF16))

    HFM = 0
    HTM = 16384
    WDQ = 32768
    WUQo = HTM
    WUKVo = WDQ
    CQF = 49152
    CQN = 57344
    STG = 65536
    PTL = 67584
    KSB = 0
    QSB = 16384
    VSB = 32768
    OSB = 49152
    PTT = 65536
    OTM = 0
    OFM = 16384
    WOB = 32768
    H2TM = 0
    H2FM = 16384
    HEB = 32768
    W12 = 44032
    W3B = 52224
    W3B = 44032
    W12 = 66560
    W12 = 0
    SPARE5 = 8192

    XTM = 0
    SCR = 8192
    XR = 0
    X2T = 4096
    PRB = 10240
    CBC = 8192
    SIL = 9216
    T2R = 12288
    YEV = 13312
    XRL = 0
    OUTR = 4096
    YRL = 15360
    YRL = 8192

    nb = {}

    b.dma("y", idb[:, :], identb[:, :], writes=("idb",))
    b.dma("y", idf[:, :], identf[:, :], writes=("idf",))
    b.dma("y", msk[:, :], masks[:, :], writes=("msk",))
    b.dma("y", qgb[:, :], qg[:, :], writes=("qgb",))
    b.dma("y", qbb[:, :], qb[:, :], writes=("qbb",))
    b.dma("y", kgb[:, :], kg[:, :], writes=("kgb",))
    b.dma("y", kbb[:, :], kb_[:, :], writes=("kbb",))
    b.dma("g", gsb[:, :], gate[:, :], writes=("gsb",))
    b.op("v", lambda v: v.memset(onef[:, :], 1.0), writes=("onef",))
    b.op("v", lambda v: v.memset(epsb[:, :], EPS), writes=("epsb",))
    if limit <= 0.2:
        b.dbg("msk_chk", [128, 896], F32, msk[:, :], reads=("msk",))
        return _cut()

    b.dma("g", aB[:, WDQ:WDQ + 8192], wdq[:, :], writes=("wdqb",))
    b.dma("g", aB[:, WDQ + 8192:WDQ + 16384], wdkv[:, :], writes=("wdkvb",))

    for half in range(2):
        for i in range(4):
            nt = half * 4 + i
            xt = aF[:, XTM + i * 2048:XTM + (i + 1) * 2048]
            b.dma("y", xt, x_own[nt * 128:(nt + 1) * 128, :],
                  writes=(f"xtile{nt}",) + ((f"xtile{nt - 4}",) if half else ()))
        for i in range(4):
            nt = half * 4 + i
            xt = aF[:, XTM + i * 2048:XTM + (i + 1) * 2048]
            ssl = stats[:, nt:nt + 1]
            sql = stats[:, 8 + nt:8 + nt + 1]
            b.op("v", lambda v, xt=xt, ssl=ssl: v.reduce_sum(ssl, xt, axis=AX.X),
                 reads=(f"xtile{nt}",), writes=(f"s1:{nt}",))
            b.op("s", lambda s, xt=xt, sql=sql: s.activation(
                aF[:, SCR:SCR + 2048], xt, AF.Square, accum_out=sql),
                 reads=(f"xtile{nt}",), writes=(f"sq1:{nt}", "scr"))
        for i in range(4):
            nt = half * 4 + i
            ssl = stats[:, nt:nt + 1]
            sql = stats[:, 8 + nt:8 + nt + 1]
            nmu = stats[:, 16 + nt:16 + nt + 1]
            rr = stats[:, 24 + nt:24 + nt + 1]
            b.op("v", lambda v, ssl=ssl, nmu=nmu: v.tensor_scalar(
                nmu, ssl, (-1.0 / D), None, ALU.mult),
                 reads=(f"s1:{nt}",), writes=(f"nmu1:{nt}",))
            b.op("v", lambda v, sql=sql, rr=rr: v.tensor_scalar(
                rr, sql, (1.0 / D), None, ALU.mult),
                 reads=(f"sq1:{nt}",), writes=(f"r1:{nt}",))
            b.op("v", lambda v, nmu=nmu, rr=rr: v.scalar_tensor_tensor(
                rr, nmu, nmu, rr, ALU.mult, ALU.subtract),
                 reads=(f"nmu1:{nt}", f"r1:{nt}"), writes=(f"r1:{nt}",))
            b.op("s", lambda s, rr=rr: s.activation(
                rr, rr, AF.Sqrt, bias=epsb[:, 0:1], scale=FNEG1),
                 reads=(f"r1:{nt}", "epsb"), writes=(f"r1:{nt}",))
            b.op("v", lambda v, rr=rr: v.reciprocal(rr, rr),
                 reads=(f"r1:{nt}",), writes=(f"r1:{nt}",))
        for i in range(4):
            nt = half * 4 + i
            xt = aF[:, XTM + i * 2048:XTM + (i + 1) * 2048]
            ht = aB[:, HTM + nt * 2048:HTM + (nt + 1) * 2048]
            nmu = stats[:, 16 + nt:16 + nt + 1]
            rr = stats[:, 24 + nt:24 + nt + 1]
            b.op("v", lambda v, ht=ht, xt=xt, nmu=nmu, rr=rr: v.tensor_scalar(
                ht, xt, nmu, rr, ALU.add, ALU.mult),
                 reads=(f"xtile{nt}", f"nmu1:{nt}", f"r1:{nt}"),
                 writes=(f"htm{nt}",))
    if limit <= 0.4:
        b.dbg("htm_chk", [128, 16384], BF16, aB[:, HTM:HTM + 16384],
              reads=tuple(f"htm{nt}" for nt in range(NT)))
        return _cut()

    for cc in range(CCD):
        ph = cc % 2
        pts = PT[:, ph * 1024:(ph + 1) * 1024]
        def tfn(t, cc=cc, pts=pts):
            instr = None
            for nt2 in range(NT):
                instr = t.transpose(
                    pts[:, nt2 * 128:(nt2 + 1) * 128],
                    aB[:, HTM + nt2 * 2048 + cc * 128:HTM + nt2 * 2048 + (cc + 1) * 128],
                    idb[:, :])
            return instr
        b.op("t", tfn,
             reads=tuple(f"htm{nt2}" for nt2 in range(NT)) + ("idb",),
             writes=(f"PTh{ph}",))
        hf = aB[:, HFM + cc * 1024:HFM + (cc + 1) * 1024]
        b.op("v", lambda v, hf=hf, pts=pts: v.tensor_copy(hf, pts),
             reads=(f"PTh{ph}",), writes=(f"hfm{cc}",))
    b.dbg("h_fm", [128, 16384], BF16, aB[:, HFM:HFM + 16384],
          reads=tuple(f"hfm{c}" for c in range(CCD)))
    if limit <= 0.6:
        return _cut()

    hfm_reads = tuple(f"hfm{c}" for c in range(CCD))
    for proj, wbkey, woff, gbuf, bbuf, cqnoff, tag in (
        ("q", "wdqb", WDQ, qgb, qbb, CQN, "cq"),
        ("k", "wdkvb", WDQ + 8192, kgb, kbb, CQN + 4096, "ckv"),
    ):
        for nt in range(NT):
            acc = P1 if nt % 2 == 0 else P2
            acck = ("P1" if nt % 2 == 0 else "P2")
            pairs = []
            for kc in range(CCD):
                lh = aB[:, HFM + kc * 1024 + nt * 128:HFM + kc * 1024 + nt * 128 + 128]
                rh = aB[:, woff + kc * 512:woff + (kc + 1) * 512]
                pairs.append((lh, rh))
            b.mmgroup(pairs, acc[:, :], reads=hfm_reads + (wbkey,), writes=(acck,))
            if limit <= 0.7 and proj == "q" and nt == 0:
                b.op("v", lambda v, acc=acc: v.tensor_copy(
                    aF[:, SCR:SCR + 512], acc[:, :]),
                     reads=(acck,), writes=("scr",))
                b.dbg("cqpre0", [128, 512], F32, aF[:, SCR:SCR + 512],
                      reads=("scr",))
                return _cut()
            base = 32 + (nt % 2) * 16
            bn6 = stats[:, base:base + 6]
            mv = stats[:, base + 6:base + 8]
            rr = stats[:, base + 8:base + 9]
            b.op("v", lambda v, bn6=bn6, acc=acc: v.bn_stats(bn6, acc[:, :]),
                 reads=(acck,), writes=(f"bn:{tag}{nt}",))
            b.op("v", lambda v, mv=mv, bn6=bn6: v.bn_aggr(mv, bn6),
                 reads=(f"bn:{tag}{nt}",), writes=(f"mv:{tag}{nt}",))
            b.op("s", lambda s, rr=rr, mv=mv: s.activation(
                rr, mv[:, 1:2], AF.Sqrt, bias=epsb[:, 0:1], scale=F1),
                 reads=(f"mv:{tag}{nt}", "epsb"), writes=(f"r2:{tag}{nt}",))
            b.op("v", lambda v, rr=rr: v.reciprocal(rr, rr),
                 reads=(f"r2:{tag}{nt}",), writes=(f"r2:{tag}{nt}",))
            cqn = aB[:, cqnoff + nt * 512:cqnoff + (nt + 1) * 512]
            b.op("v", lambda v, cqn=cqn, acc=acc, mv=mv, rr=rr: v.tensor_scalar(
                cqn, acc[:, :], mv[:, 0:1], rr, ALU.subtract, ALU.mult),
                 reads=(acck, f"mv:{tag}{nt}", f"r2:{tag}{nt}"),
                 writes=(f"{tag}n{nt}", acck))
            if limit <= 0.8 and proj == "q" and nt == 0:
                b.dbg("stat_chk", [128, 64], F32, stats[:, :],
                      reads=(f"mv:{tag}0",))
                b.dbg("cqn0", [128, 512], BF16, cqn, reads=(f"{tag}n{nt}",))
                return _cut()
        fmoff = CQF if tag == "cq" else CQF + 4096
        for q2 in range(4):
            ph = q2 % 2
            pts = PT[:, ph * 1024:(ph + 1) * 1024]
            def tfn(t, q2=q2, pts=pts, cqnoff=cqnoff):
                instr = None
                for nt2 in range(NT):
                    instr = t.transpose(
                        pts[:, nt2 * 128:(nt2 + 1) * 128],
                        aB[:, cqnoff + nt2 * 512 + q2 * 128:cqnoff + nt2 * 512 + (q2 + 1) * 128],
                        idb[:, :])
                return instr
            b.op("t", tfn,
                 reads=tuple(f"{tag}n{nt2}" for nt2 in range(NT)) + ("idb",),
                 writes=(f"PTh{ph}",))
            dst = aB[:, fmoff + q2 * 1024:fmoff + (q2 + 1) * 1024]
            gb = gbuf[:, q2:q2 + 1]
            bb2 = bbuf[:, q2:q2 + 1]
            gkey = "qgb" if tag == "cq" else "kgb"
            bkey = "qbb" if tag == "cq" else "kbb"
            b.op("v", lambda v, dst=dst, pts=pts, gb=gb, bb2=bb2: v.tensor_scalar(
                dst, pts, gb, bb2, ALU.mult, ALU.add),
                 reads=(f"PTh{ph}", gkey, bkey), writes=(f"{tag}fm{q2}",))
    b.dbg("cq_fm", [128, 4096], BF16, aB[:, CQF:CQF + 4096],
          reads=tuple(f"cqfm{q2}" for q2 in range(4)))
    b.dbg("ckv_fm", [128, 4096], BF16, aB[:, CQF + 4096:CQF + 8192],
          reads=tuple(f"ckvfm{q2}" for q2 in range(4)))
    if limit <= 1:
        return _cut()

    b.dma("g", aB[:, WUQo:WUQo + 8192], wuq[:, :],
          writes=("wuqb",) + tuple(f"htm{nt}" for nt in range(NT)))
    b.dma("g", aB[:, WUKVo:WUKVo + 16384], wukv[:, :],
          writes=("wukvb", "wdqb", "wdkvb"))

    cqfm_reads = tuple(f"cqfm{q2}" for q2 in range(4))
    ckvfm_reads = tuple(f"ckvfm{q2}" for q2 in range(4))

    stg_state = {"i": 0}

    def stage_out(psum_ap, psum_key, scale, dst_ap, dkey):
        sl = stg_state["i"] % 4
        stg_state["i"] += 1
        st = aB[:, STG + sl * 512:STG + (sl + 1) * 512]
        skey = f"stg{sl}"
        if scale is None:
            b.op("v", lambda v, st=st, p=psum_ap: v.tensor_copy(st, p),
                 reads=(psum_key, skey), writes=(skey, psum_key))
        else:
            b.op("v", lambda v, st=st, p=psum_ap: v.tensor_scalar(
                st, p, scale, None, ALU.mult),
                 reads=(psum_key, skey), writes=(skey, psum_key))
        b.dma("y", dst_ap, st, reads=(skey,), writes=(dkey,))

    for oc in range(CCD):
        for ng in range(2):
            acc = P1 if (oc * 2 + ng) % 2 == 0 else P2
            acck = "P1" if (oc * 2 + ng) % 2 == 0 else "P2"
            pairs = []
            for kc in range(4):
                lh = aB[:, WUKVo + kc * 4096 + oc * 128:WUKVo + kc * 4096 + (oc + 1) * 128]
                rh = aB[:, CQF + 4096 + kc * 1024 + ng * 512:CQF + 4096 + kc * 1024 + (ng + 1) * 512]
                pairs.append((lh, rh))
            b.mmgroup(pairs, acc[:, :], reads=ckvfm_reads + ("wukvb",), writes=(acck,))
            d = ng * 4 + oc // 4
            off = (oc % 4) * 128 * 512
            dst = cc_in[d, off:off + 128 * 512].rearrange("(p n) -> p n", p=128)
            stage_out(acc[:, :], acck, None, dst, f"ccinK{d}:{oc % 4}")
    for nt in range(NT):
        for og in range(4):
            acc = P1 if (nt * 4 + og) % 2 == 0 else P2
            acck = "P1" if (nt * 4 + og) % 2 == 0 else "P2"
            pairs = []
            for kc in range(4):
                lh = aB[:, CQF + 4096 + kc * 1024 + nt * 128:CQF + 4096 + kc * 1024 + nt * 128 + 128]
                rh = aB[:, WUKVo + kc * 4096 + 2048 + og * 512:WUKVo + kc * 4096 + 2048 + (og + 1) * 512]
                pairs.append((lh, rh))
            b.mmgroup(pairs, acc[:, :], reads=ckvfm_reads + ("wukvb",), writes=(acck,))
            ci = nt // 2
            d = (ci // 2) * 4 + og
            roff = (ci % 2) * 256 + (nt % 2) * 128
            off = 512 * 512 + roff * 512
            dst = cc_in[d, off:off + 128 * 512].rearrange("(p n) -> p n", p=128)
            stage_out(acc[:, :], acck, None, dst, f"ccinV{d}:{roff // 128}")
    for oc in range(CCD):
        for ng in range(2):
            acc = P1 if (oc * 2 + ng) % 2 == 0 else P2
            acck = "P1" if (oc * 2 + ng) % 2 == 0 else "P2"
            pairs = []
            for kc in range(4):
                lh = aB[:, WUQo + kc * 2048 + oc * 128:WUQo + kc * 2048 + (oc + 1) * 128]
                rh = aB[:, CQF + kc * 1024 + ng * 512:CQF + kc * 1024 + (ng + 1) * 512]
                pairs.append((lh, rh))
            b.mmgroup(pairs, acc[:, :], reads=cqfm_reads + ("wuqb",), writes=(acck,))
            d = ng * 4 + oc // 4
            off = 2 * 512 * 512 + (oc % 4) * 128 * 512
            dst = cc_in[d, off:off + 128 * 512].rearrange("(p n) -> p n", p=128)
            stage_out(acc[:, :], acck, ISC, dst, f"ccinQ{d}:{oc % 4}")

    ccin_keys = tuple(f"ccinK{d}:{i}" for d in range(8) for i in range(4)) + \
        tuple(f"ccinV{d}:{i}" for d in range(8) for i in range(4)) + \
        tuple(f"ccinQ{d}:{i}" for d in range(8) for i in range(4))
    b.pre("g", ccin_keys, ("ccout",))
    cc_instr = nc.gpsimd.collective_compute(
        "AllToAll", ALU.bypass, replica_groups=[list(range(NCORE))],
        ins=[cc_in[:, :]], outs=[cc_out[:, :]])
    cc_instr.then_inc(b.sem["cc"], 1)
    b.cnt["cc"] += 1
    r = b._r("ccout"); r["w"] = ("cc", b.cnt["cc"]); r["r"] = {}
    for k in ccin_keys:
        b._r(k)["r"]["cc"] = b.cnt["cc"]

    for j in range(NCORE):
        blk = cc_out[j, :]
        for which, base, tag2 in (("K", 0, "ksb"), ("Q", 2 * 512 * 512, "qsb")):
            for hc in range(4):
                for half in range(2):
                    src = blk[base + hc * 128 * 512 + half * 256:
                              base + (hc * 128 + 127) * 512 + half * 256 + 256]
                    src2 = blk[base:base + 512 * 512].rearrange(
                        "(d t) -> d t", t=512)[hc * 128:(hc + 1) * 128,
                                               half * 256:(half + 1) * 256]
                    p = (j if half == 0 else 15 - j)
                    dsto = (KSB if which == "K" else QSB) + hc * 4096 + p * 256
                    b.dma("s", aB[:, dsto:dsto + 256], src2,
                          reads=("ccout",), writes=(f"{tag2}{hc}:{p}",))
        for half in range(2):
            for sub in range(2):
                src2 = blk[512 * 512:2 * 512 * 512].rearrange(
                    "(t d) -> t d", d=512)[half * 256 + sub * 128:
                                           half * 256 + (sub + 1) * 128, :]
                p = (j if half == 0 else 15 - j)
                vt = p * 2 + sub
                b.dma("s", aB[:, VSB + vt * 512:VSB + (vt + 1) * 512], src2,
                      reads=("ccout",), writes=(f"vsb{vt}",))

    if limit <= 2:
        b.dbg("ksb", [128, 16384], BF16, aB[:, KSB:KSB + 16384],
              reads=tuple(f"ksb{hc}:{p}" for hc in range(4) for p in range(16)))
        b.dbg("qsb", [128, 16384], BF16, aB[:, QSB:QSB + 16384],
              reads=tuple(f"qsb{hc}:{p}" for hc in range(4) for p in range(16)))
        b.dbg("vsb", [128, 16384], BF16, aB[:, VSB:VSB + 16384],
              reads=tuple(f"vsb{vt}" for vt in range(32)))
        return _cut()

    for qc in range(16):
        nkb = ceil_div(qc + 1, 2)
        passes = [(0, min(4, nkb))]
        if nkb > 4:
            passes.append((4, nkb))
        for qt in range(2):
            qcol = qc * 256 + qt * 128
            for hh in range(4):
                qread = (f"qsb{hh}:{p}" for p in range(qc + 1))
                lh = aB[:, QSB + hh * 4096 + qcol:QSB + hh * 4096 + qcol + 128]
                npass = len(passes)
                for pi, (k0, k1) in enumerate(passes):
                    nk = k1 - k0
                    kreads = tuple(f"ksb{hh}:{p}" for p in range(2 * k0, 2 * k1))
                    def sfn(t, lh=lh, k0=k0, k1=k1, hh=hh):
                        instr = None
                        for kk in range(k1 - k0):
                            rh = aB[:, KSB + hh * 4096 + (k0 + kk) * 512:
                                    KSB + hh * 4096 + (k0 + kk + 1) * 512]
                            instr = t.matmul(P0[:, kk * 512:(kk + 1) * 512], lh, rh,
                                             start=True, stop=True)
                        return instr
                    p0keys = tuple(f"P0:{i}" for i in range(nk))
                    b.op("t", sfn, reads=kreads + (f"qsb{hh}:{qc}",),
                         writes=p0keys)
                    if pi == npass - 1:
                        doff = (0 if qc % 2 == 0 else 256) + qt * 128
                        b.op("v", lambda v, nk=nk, doff=doff: v.tensor_tensor(
                            P0[:, (nk - 1) * 512:nk * 512],
                            P0[:, (nk - 1) * 512:nk * 512],
                            msk[:, 384 - doff:896 - doff], ALU.add),
                             reads=(f"P0:{nk-1}", "msk"), writes=(f"P0:{nk-1}",))
                    nm = lbuf[:, pi:pi + 1]
                    b.op("v", lambda v, nm=nm, nk=nk: v.reduce_max(
                        nm, P0[:, :nk * 512], axis=AX.X, negate=True),
                         reads=p0keys, writes=(f"nm{pi}",))
                    ll = lbuf[:, 4 + pi:5 + pi]
                    b.op("s", lambda s, nm=nm, ll=ll, nk=nk: s.activation(
                        ptl[:, 0:nk * 512], P0[:, :nk * 512], AF.Exp,
                        bias=nm, scale=F1, accum_out=ll),
                         reads=p0keys + (f"nm{pi}",), writes=("ptl", f"l{pi}") + p0keys)
                    nkt = nk * 4
                    for g0 in range(0, nkt, 8):
                        g1 = min(g0 + 8, nkt)
                        ph = (g0 // 8) % 2
                        pts = PT[:, ph * 1024:(ph + 1) * 1024]
                        def tfn(t, g0=g0, g1=g1, pts=pts):
                            instr = None
                            for kt in range(g0, g1):
                                instr = t.transpose(
                                    pts[:, (kt - g0) * 128:(kt - g0 + 1) * 128],
                                    ptl[:, kt * 128:(kt + 1) * 128],
                                    idb[:, :])
                            return instr
                        b.op("t", tfn, reads=("ptl", "idb"), writes=(f"PTh{ph}",))
                        b.op("v", lambda v, g0=g0, g1=g1, pts=pts: v.tensor_copy(
                            aB[:, PTT + g0 * 128:PTT + g1 * 128],
                            pts[:, :(g1 - g0) * 128]),
                             reads=(f"PTh{ph}",), writes=("ptt",))
                    vreads = tuple(f"vsb{vt}" for vt in range(4 * k0, 4 * k1))
                    def pvfn(t, k0=k0, k1=k1, hh=hh, nkt=nkt):
                        instr = None
                        n = nkt
                        for kt in range(n):
                            vt = k0 * 4 + kt
                            lh2 = aB[:, PTT + kt * 128:PTT + (kt + 1) * 128]
                            rh2 = aB[:, VSB + vt * 512 + hh * 128:VSB + vt * 512 + (hh + 1) * 128]
                            instr = t.matmul(P1[:, 0:128], lh2, rh2,
                                             start=(kt == 0), stop=(kt == n - 1))
                        return instr
                    b.op("t", pvfn, reads=("ptt",) + vreads, writes=("P1",))
                    if npass == 2 and pi == 0:
                        o0 = o0f[:, hh * 256 + qt * 128:hh * 256 + (qt + 1) * 128]
                        b.op("v", lambda v, o0=o0: v.tensor_copy(o0, P1[:, 0:128]),
                             reads=("P1",), writes=("o0f", "P1"))
                otile = qc * 2 + qt
                odst = aB[:, OSB + otile * 512 + hh * 128:OSB + otile * 512 + (hh + 1) * 128]
                if npass == 1:
                    rl = lbuf[:, 8:9]
                    b.op("v", lambda v, rl=rl: v.reciprocal(rl, lbuf[:, 4:5]),
                         reads=("l0",), writes=("rl",))
                    b.op("v", lambda v, odst=odst, rl=rl: v.tensor_scalar(
                        odst, P1[:, 0:128], rl, None, ALU.mult),
                         reads=("P1", "rl"), writes=(f"osb{otile}:{hh}", "P1"))
                else:
                    nmf = lbuf[:, 9:10]; a0 = lbuf[:, 10:11]; a1 = lbuf[:, 11:12]
                    lf = lbuf[:, 12:13]; rl = lbuf[:, 8:9]
                    t2c = lbuf[:, 13:14]
                    b.op("v", lambda v, nmf=nmf: v.tensor_tensor(
                        nmf, lbuf[:, 0:1], lbuf[:, 1:2], ALU.min),
                         reads=("nm0", "nm1"), writes=("nmf",))
                    b.op("s", lambda s, a0=a0, nmf=nmf: s.activation(
                        a0, lbuf[:, 0:1], AF.Exp, bias=nmf, scale=FNEG1),
                         reads=("nm0", "nmf"), writes=("a0",))
                    b.op("s", lambda s, a1=a1, nmf=nmf: s.activation(
                        a1, lbuf[:, 1:2], AF.Exp, bias=nmf, scale=FNEG1),
                         reads=("nm1", "nmf"), writes=("a1",))
                    b.op("v", lambda v, t2c=t2c, a1=a1: v.tensor_tensor(
                        t2c, lbuf[:, 5:6], a1, ALU.mult),
                         reads=("l1", "a1"), writes=("t2c",))
                    b.op("v", lambda v, lf=lf, a0=a0, t2c=t2c: v.scalar_tensor_tensor(
                        lf, lbuf[:, 4:5], a0, t2c, ALU.mult, ALU.add),
                         reads=("l0", "a0", "t2c"), writes=("lf",))
                    b.op("v", lambda v, rl=rl, lf=lf: v.reciprocal(rl, lf),
                         reads=("lf",), writes=("rl",))
                    o0 = o0f[:, hh * 256 + qt * 128:hh * 256 + (qt + 1) * 128]
                    b.op("v", lambda v, a1=a1: v.tensor_scalar(
                        P1[:, 128:256], P1[:, 0:128], a1, None, ALU.mult),
                         reads=("P1", "a1"), writes=("P1",))
                    b.op("v", lambda v, o0=o0, a0=a0: v.scalar_tensor_tensor(
                        P1[:, 256:384], o0, a0, P1[:, 128:256], ALU.mult, ALU.add),
                         reads=("o0f", "a0", "P1"), writes=("P1",))
                    b.op("v", lambda v, odst=odst, rl=rl: v.tensor_scalar(
                        odst, P1[:, 256:384], rl, None, ALU.mult),
                         reads=("P1", "rl"), writes=(f"osb{otile}:{hh}", "P1"))

    if limit <= 3:
        b.dbg("osb", [128, 16384], BF16, aB[:, OSB:OSB + 16384],
              reads=tuple(f"osb{t2}:{hh}" for t2 in range(32) for hh in range(4)))
        return _cut()

    for tile in range(32):
        p = tile // 2
        sub = tile % 2
        d = p if p < 8 else 15 - p
        roff = (0 if p < 8 else 256) + sub * 128
        dst = cc2_in[d, roff * 512:(roff + 128) * 512].rearrange("(p n) -> p n", p=128)
        b.dma("y", dst, aB[:, OSB + tile * 512:OSB + (tile + 1) * 512],
              reads=tuple(f"osb{tile}:{hh}" for hh in range(4)),
              writes=(f"cc2in{tile}",))
    b.pre("g", tuple(f"cc2in{t2}" for t2 in range(32)), ("cc2out",))
    cc2 = nc.gpsimd.collective_compute(
        "AllToAll", ALU.bypass, replica_groups=[list(range(NCORE))],
        ins=[cc2_in[:, :]], outs=[cc2_out[:, :]])
    cc2.then_inc(b.sem["cc"], 1)
    b.cnt["cc"] += 1
    r = b._r("cc2out"); r["w"] = ("cc", b.cnt["cc"]); r["r"] = {}
    for t2 in range(32):
        b._r(f"cc2in{t2}")["r"]["cc"] = b.cnt["cc"]

    for nt in range(NT):
        ci = nt // 2
        bsel = ci // 2
        roff = (ci % 2) * 256 + (nt % 2) * 128
        for hg in range(4):
            s2 = bsel * 4 + hg
            src = cc2_out[s2, roff * 512:(roff + 128) * 512].rearrange(
                "(p n) -> p n", p=128)
            b.dma("y", aB[:, OTM + nt * 2048 + hg * 512:OTM + nt * 2048 + (hg + 1) * 512],
                  src, reads=("cc2out",), writes=(f"otm{nt}",))
    b.dbg("o_tm", [128, 16384], BF16, aB[:, OTM:OTM + 16384],
          reads=tuple(f"otm{nt}" for nt in range(NT)))
    for jc in range(CCD):
        ph = jc % 2
        pts = PT[:, ph * 1024:(ph + 1) * 1024]
        def tfn(t, jc=jc, pts=pts):
            instr = None
            for nt2 in range(NT):
                instr = t.transpose(
                    pts[:, nt2 * 128:(nt2 + 1) * 128],
                    aB[:, OTM + nt2 * 2048 + jc * 128:OTM + nt2 * 2048 + (jc + 1) * 128],
                    idb[:, :])
            return instr
        b.op("t", tfn, reads=tuple(f"otm{nt2}" for nt2 in range(NT)) + ("idb",),
             writes=(f"PTh{ph}",))
        b.op("v", lambda v, jc=jc, pts=pts: v.tensor_copy(
            aB[:, OFM + jc * 1024:OFM + (jc + 1) * 1024], pts),
             reads=(f"PTh{ph}",), writes=(f"ofm{jc}",))
    for jc in range(CCD):
        b.dma("g", aB[:, WOB + jc * 2048:WOB + (jc + 1) * 2048],
              wo[:, jc * 2048:(jc + 1) * 2048],
              reads=(), writes=(f"wob{jc}",))
    ofm_reads = tuple(f"ofm{jc}" for jc in range(CCD))
    for nt in range(NT):
        xr = aF[:, XR + (nt % 2) * 2048:XR + (nt % 2 + 1) * 2048]
        b.dma("y", xr, x_own[nt * 128:(nt + 1) * 128, :],
              writes=(f"xrl{nt}",) + ((f"xrl{nt - 2}",) if nt >= 2 else ()))
        x2t = aF[:, X2T + (nt % 2) * 2048:X2T + (nt % 2 + 1) * 2048]
        for ig in range(4):
            acc = P1 if ig % 2 == 0 else P2
            acck = "P1" if ig % 2 == 0 else "P2"
            pairs = []
            for jc in range(CCD):
                lh = aB[:, OFM + jc * 1024 + nt * 128:OFM + jc * 1024 + nt * 128 + 128]
                rh = aB[:, WOB + jc * 2048 + ig * 512:WOB + jc * 2048 + (ig + 1) * 512]
                pairs.append((lh, rh))
            b.mmgroup(pairs, acc[:, :],
                      reads=ofm_reads + tuple(f"wob{jc}" for jc in range(CCD)),
                      writes=(acck,))
            b.op("v", lambda v, x2t=x2t, acc=acc, ig=ig, xr=xr: v.tensor_tensor(
                x2t[:, ig * 512:(ig + 1) * 512], acc[:, :],
                xr[:, ig * 512:(ig + 1) * 512], ALU.add),
                 reads=(acck, f"xrl{nt}"), writes=(f"x2t{nt % 2}:{ig}", acck))
        x2keys = tuple(f"x2t{nt % 2}:{ig}" for ig in range(4))
        b.dma("y", x2h[nt * 128:(nt + 1) * 128, :], x2t,
              reads=x2keys, writes=(f"x2h{nt}",))
        ssl = stats[:, nt:nt + 1]; sql = stats[:, 8 + nt:8 + nt + 1]
        nmu = stats[:, 16 + nt:16 + nt + 1]; rr = stats[:, 24 + nt:24 + nt + 1]
        b.op("v", lambda v, ssl=ssl, x2t=x2t: v.reduce_sum(ssl, x2t, axis=AX.X),
             reads=x2keys, writes=(f"s4:{nt}",))
        b.op("s", lambda s, sql=sql, x2t=x2t: s.activation(
            aF[:, SCR:SCR + 2048], x2t, AF.Square, accum_out=sql),
             reads=x2keys, writes=(f"sq4:{nt}", "scr"))
        b.op("v", lambda v, ssl=ssl, nmu=nmu: v.tensor_scalar(
            nmu, ssl, (-1.0 / D), None, ALU.mult),
             reads=(f"s4:{nt}",), writes=(f"nmu4:{nt}",))
        b.op("v", lambda v, sql=sql, rr=rr: v.tensor_scalar(
            rr, sql, (1.0 / D), None, ALU.mult),
             reads=(f"sq4:{nt}",), writes=(f"r4:{nt}",))
        b.op("v", lambda v, nmu=nmu, rr=rr: v.scalar_tensor_tensor(
            rr, nmu, nmu, rr, ALU.mult, ALU.subtract),
             reads=(f"nmu4:{nt}", f"r4:{nt}"), writes=(f"r4:{nt}",))
        b.op("s", lambda s, rr=rr: s.activation(
            rr, rr, AF.Sqrt, bias=epsb[:, 0:1], scale=FNEG1),
             reads=(f"r4:{nt}", "epsb"), writes=(f"r4:{nt}",))
        b.op("v", lambda v, rr=rr: v.reciprocal(rr, rr),
             reads=(f"r4:{nt}",), writes=(f"r4:{nt}",))
        ht = aB[:, H2TM + nt * 2048:H2TM + (nt + 1) * 2048]
        b.op("v", lambda v, ht=ht, x2t=x2t, nmu=nmu, rr=rr: v.tensor_scalar(
            ht, x2t, nmu, rr, ALU.add, ALU.mult),
             reads=x2keys + (f"nmu4:{nt}", f"r4:{nt}"),
             writes=(f"h2tm{nt}", f"x2t{nt % 2}:0", f"x2t{nt % 2}:1",
                     f"x2t{nt % 2}:2", f"x2t{nt % 2}:3"))
    if "x2" in b.dbg_names:
        t_x2 = nc.dram_tensor("dbg_x2", [T, D], F32, kind="ExternalOutput").ap()
        b.dbg_tensors["x2"] = t_x2
        b.dma("y", t_x2[:, :], x2h[:, :],
              reads=tuple(f"x2h{nt}" for nt in range(NT)), writes=("dbg_x2",))

    for cc in range(CCD):
        ph = cc % 2
        pts = PT[:, ph * 1024:(ph + 1) * 1024]
        def tfn(t, cc=cc, pts=pts):
            instr = None
            for nt2 in range(NT):
                instr = t.transpose(
                    pts[:, nt2 * 128:(nt2 + 1) * 128],
                    aB[:, H2TM + nt2 * 2048 + cc * 128:H2TM + nt2 * 2048 + (cc + 1) * 128],
                    idb[:, :])
            return instr
        b.op("t", tfn, reads=tuple(f"h2tm{nt2}" for nt2 in range(NT)) + ("idb",),
             writes=(f"PTh{ph}",))
        b.op("v", lambda v, cc=cc, pts=pts: v.tensor_copy(
            aB[:, H2FM + cc * 1024:H2FM + (cc + 1) * 1024], pts),
             reads=(f"PTh{ph}",), writes=(f"h2fm{cc}",))
    h2fm_reads = tuple(f"h2fm{cc}" for cc in range(CCD))
    b.dbg("h2_fm", [128, 16384], BF16, aB[:, H2FM:H2FM + 16384], reads=h2fm_reads)
    if limit <= 4:
        return _cut()

    for nt in range(NT):
        pairs = []
        for kc in range(CCD):
            lh = aB[:, H2FM + kc * 1024 + nt * 128:H2FM + kc * 1024 + nt * 128 + 128]
            rh = gsb[:, kc * 8:(kc + 1) * 8]
            pairs.append((lh, rh))
        b.mmgroup(pairs, P0[:, 0:8], reads=h2fm_reads + ("gsb",), writes=("P0:0",))
        prb = gprb[:, nt * 8:nt * 8 + 8]
        wrk = aF[:, PRB + nt * 64 + 8:PRB + nt * 64 + 16]
        cmb = aF[:, PRB + nt * 64 + 16:PRB + nt * 64 + 24]
        nm = lbuf[:, 16:17]; sm = lbuf[:, 17:18]; rs = lbuf[:, 18:19]
        m1 = lbuf[:, 19:20]; m2 = lbuf[:, 20:21]
        b.op("v", lambda v, nm=nm: v.reduce_max(nm, P0[:, 0:8], axis=AX.X, negate=True),
             reads=("P0:0",), writes=("gnm",))
        b.op("s", lambda s, prb=prb, nm=nm, sm=sm: s.activation(
            prb, P0[:, 0:8], AF.Exp, bias=nm, scale=F1, accum_out=sm),
             reads=("P0:0", "gnm"), writes=(f"gprb{nt}", "gsm", "P0:0"))
        b.op("v", lambda v, rs=rs, sm=sm: v.reciprocal(rs, sm),
             reads=("gsm",), writes=("grs",))
        b.op("v", lambda v, prb=prb, rs=rs: v.tensor_scalar(
            prb, prb, rs, None, ALU.mult),
             reads=(f"gprb{nt}", "grs"), writes=(f"gprb{nt}",))
        b.op("v", lambda v, m1=m1, prb=prb: v.reduce_max(m1, prb, axis=AX.X),
             reads=(f"gprb{nt}",), writes=("gm1",))
        b.op("v", lambda v, wrk=wrk, prb=prb, m1=m1: v.tensor_scalar(
            wrk, prb, m1, None, ALU.is_ge),
             reads=(f"gprb{nt}", "gm1"), writes=(f"gwrk{nt}",))
        b.op("v", lambda v, wrk=wrk, prb=prb: v.scalar_tensor_tensor(
            wrk, wrk, (-1e30), prb, ALU.mult, ALU.add),
             reads=(f"gwrk{nt}", f"gprb{nt}"), writes=(f"gwrk{nt}",))
        b.op("v", lambda v, m2=m2, wrk=wrk: v.reduce_max(m2, wrk, axis=AX.X),
             reads=(f"gwrk{nt}",), writes=("gm2",))
        b.op("v", lambda v, cmb=cmb, prb=prb, m2=m2: v.tensor_scalar(
            cmb, prb, m2, None, ALU.is_ge),
             reads=(f"gprb{nt}", "gm2"), writes=(f"gcmb{nt}",))
        b.op("v", lambda v, cmb=cmb, prb=prb: v.tensor_tensor(
            cmb, cmb, prb, ALU.mult),
             reads=(f"gcmb{nt}", f"gprb{nt}"), writes=(f"gcmb{nt}",))
        b.op("t", lambda t, cmb=cmb, nt=nt: t.transpose(
            P0[0:8, 1024 + nt * 128:1024 + (nt + 1) * 128], cmb, idf[:, :]),
             reads=(f"gcmb{nt}", "idf"), writes=("P0comb",))
    b.op("v", lambda v: v.tensor_copy(combr[:, :], P0[0:8, 1024:2048]),
         reads=("P0comb",), writes=("combr",))
    b.dbg("comb_rows", [8, 1024], F32, combr[:, :], reads=("combr",))
    cflat1 = aF[0:1, 14336:15360]
    if limit <= 5:
        return _cut()

    def w12_load(w1s, w2s, ic, claims):
        sl = ic % 2
        w1b = aB[:, W12 + sl * 2048:W12 + (sl + 1) * 2048]
        w2b = aB[:, W12 + 4096 + sl * 2048:W12 + 4096 + (sl + 1) * 2048]
        cl1 = (f"h2tm{sl}",) if claims else ()
        cl2 = (f"h2tm{2 + sl}",) if claims else ()
        b.dma("g", w1b, w1s[ic, :, :], writes=(f"w1b{sl}",) + cl1)
        b.dma("g", w2b, w2s[ic, :, :], writes=(f"w2b{sl}",) + cl2)

    def expert_pass(eidx, n_ic, w1src, w2src, w3src, use_comb, tagp, nxt=None):
        first = (eidx == 0)
        if first:
            w12_load(w1src, w2src, 0, True)
            w12_load(w1src, w2src, 1, True)
        if use_comb:
            b.dma("y", cflat1[0:1, :], combr[eidx:eidx + 1, :],
                  reads=("combr",), writes=("cflat",))
            def cbfn(t):
                instr = None
                for ng in range(2):
                    instr = t.matmul(P0[:, 1024 + ng * 512:1024 + (ng + 1) * 512],
                                     onef[:, :],
                                     cflat1[0:1, ng * 512:(ng + 1) * 512],
                                     start=True, stop=True)
                return instr
            b.op("t", cbfn, reads=("onef", "cflat"), writes=("P0:2", "P0:3"))
            b.op("v", lambda v: v.tensor_copy(aF[:, CBC:CBC + 1024], P0[:, 1024:2048]),
                 reads=("P0:2", "P0:3"), writes=("cbc", "P0:2", "P0:3"))
        for ic in range(n_ic):
            if ic + 1 < n_ic:
                w12_load(w1src, w2src, ic + 1, False)
            cl3 = (tuple(f"wob{jj}" for jj in range(5, 16))
                   if (first and ic == 0) else ())
            b.dma("g", aB[:, W3B + ic * 2048:W3B + (ic + 1) * 2048],
                  w3src[ic, :, :], writes=(f"w3b{ic}",) + cl3)
            sl = ic % 2
            w1b = aB[:, W12 + sl * 2048:W12 + (sl + 1) * 2048]
            w2b = aB[:, W12 + 4096 + sl * 2048:W12 + 4096 + (sl + 1) * 2048]
            for ng in range(2):
                accs = [(P1, "P1"), (P2, "P2")]
                (a1, k1), (a2, k2) = accs
                pairs1 = []
                pairs2 = []
                for kc in range(CCD):
                    lh1 = w1b[:, kc * 128:(kc + 1) * 128]
                    lh2 = w2b[:, kc * 128:(kc + 1) * 128]
                    rh = aB[:, H2FM + kc * 1024 + ng * 512:H2FM + kc * 1024 + (ng + 1) * 512]
                    pairs1.append((lh1, rh))
                    pairs2.append((lh2, rh))
                b.mmgroup(pairs1, a1[:, :], reads=h2fm_reads + (f"w1b{sl}",), writes=(k1,))
                b.mmgroup(pairs2, a2[:, :], reads=h2fm_reads + (f"w2b{sl}",), writes=(k2,))
                sil = aF[:, SIL + ng * 512:SIL + (ng + 1) * 512]
                b.op("s", lambda s, sil=sil, a1=a1: s.activation(sil, a1[:, :], AF.Silu),
                     reads=(k1,), writes=(f"sil{ng}", k1))
                t2 = aF[:, T2R + ng * 512:T2R + (ng + 1) * 512]
                b.op("v", lambda v, t2=t2, a2=a2, sil=sil: v.tensor_tensor(
                    t2, a2[:, :], sil, ALU.mult),
                     reads=(k2, f"sil{ng}"), writes=(f"t2r{ng}", k2))
                hdst = aB[:, HEB + ic * 1024 + ng * 512:HEB + ic * 1024 + (ng + 1) * 512]
                if use_comb:
                    b.op("g", lambda g, hdst=hdst, t2=t2, ng=ng: g.tensor_tensor(
                        hdst, t2, aF[:, CBC + ng * 512:CBC + (ng + 1) * 512], ALU.mult),
                         reads=(f"t2r{ng}", "cbc"), writes=(f"heb{ic}",))
                else:
                    b.op("g", lambda g, hdst=hdst, t2=t2: g.tensor_copy(hdst, t2),
                         reads=(f"t2r{ng}",), writes=(f"heb{ic}",))
        if nxt is not None:
            w12_load(nxt[0], nxt[1], 0, False)
            w12_load(nxt[0], nxt[1], 1, False)
        heb_reads = tuple(f"heb{ic}" for ic in range(n_ic))
        w3_reads = tuple(f"w3b{ic}" for ic in range(n_ic))
        for nt in range(NT):
            for cg in range(4):
                pslot = (nt * 4 + cg) % 4
                acc = P0[:, pslot * 512:(pslot + 1) * 512]
                acck = f"P0:{pslot}"
                pairs = []
                for ic in range(n_ic):
                    lh = aB[:, HEB + ic * 1024 + nt * 128:HEB + ic * 1024 + nt * 128 + 128]
                    rh = aB[:, W3B + ic * 2048 + cg * 512:W3B + ic * 2048 + (cg + 1) * 512]
                    pairs.append((lh, rh))
                b.mmgroup(pairs, acc, reads=heb_reads + w3_reads, writes=(acck,))
                ysl = (nt * 4 + cg) % 2
                yev = aF[:, YEV + ysl * 512:YEV + (ysl + 1) * 512]
                b.op("v", lambda v, yev=yev, acc=acc: v.tensor_copy(yev, acc),
                     reads=(acck,), writes=(f"yev{ysl}", acck))
                dst = yh[nt * 128:(nt + 1) * 128, cg * 512:(cg + 1) * 512]
                b.dma("g", dst, yev, reads=(f"yev{ysl}",),
                      writes=(f"yh{nt}:{cg}",),
                      accum_op=(ALU.add if eidx > 0 else ALU.bypass))

    srcs = [(w1t[e * NIC:(e + 1) * NIC, :, :],
             w2t[e * NIC:(e + 1) * NIC, :, :],
             w3t[e * NIC:(e + 1) * NIC, :, :]) for e in range(E)]
    srcs.append((sw1t[0:11, :, :], sw2t[0:11, :, :], sw3t[0:11, :, :]))
    srcs.append((sw1t[11:22, :, :], sw2t[11:22, :, :], sw3t[11:22, :, :]))
    for e in range(E):
        expert_pass(e, NIC, *srcs[e], use_comb=True, tagp=f"e{e}",
                    nxt=srcs[e + 1][:2])
    expert_pass(100, NISC // 2, *srcs[8], use_comb=False, tagp="sA",
                nxt=srcs[9][:2])
    expert_pass(101, NISC // 2, *srcs[9], use_comb=False, tagp="sB")

    FX = 0
    FY = 4096
    FO = 8192
    for nt in range(NT):
        sl = nt % 2
        fx = aF[:, FX + sl * 2048:FX + (sl + 1) * 2048]
        fy = aF[:, FY + sl * 2048:FY + (sl + 1) * 2048]
        fo = aF[:, FO + sl * 2048:FO + (sl + 1) * 2048]
        b.dma("y", fx, x2h[nt * 128:(nt + 1) * 128, :],
              reads=(f"x2h{nt}",), writes=(f"fx{sl}",))
        b.dma("y", fy, yh[nt * 128:(nt + 1) * 128, :],
              reads=tuple(f"yh{nt}:{cg}" for cg in range(4)), writes=(f"fy{sl}",))
        b.op("v", lambda v, fo=fo, fx=fx, fy=fy: v.tensor_tensor(
            fo, fx, fy, ALU.add),
             reads=(f"fx{sl}", f"fy{sl}"), writes=(f"fo{sl}", f"fx{sl}", f"fy{sl}"))
        b.dma("y", out[nt * 128:(nt + 1) * 128, :], fo,
              reads=(f"fo{sl}",), writes=(f"outd{nt}",))

    b.pre("y", tuple(f"outd{nt}" for nt in range(NT)), ())
    return b



def _relayout(inputs):
    g = {}
    f32 = np.float32
    g["wdq"] = np.ascontiguousarray(
        inputs["L0_attn_W_dq"].reshape(16, 128, 512).transpose(1, 0, 2).reshape(128, -1))
    g["wdkv"] = np.ascontiguousarray(
        inputs["L0_attn_W_dkv"].reshape(16, 128, 512).transpose(1, 0, 2).reshape(128, -1))
    g["wuq"] = np.ascontiguousarray(
        inputs["L0_attn_W_uq"].reshape(4, 128, 2048).transpose(1, 0, 2).reshape(128, -1))
    g["wukv"] = np.ascontiguousarray(
        inputs["L0_attn_W_ukv"].reshape(4, 128, 4096).transpose(1, 0, 2).reshape(128, -1))
    g["wo"] = np.ascontiguousarray(
        inputs["L0_attn_W_o"].T.reshape(16, 128, 2048).transpose(1, 0, 2).reshape(128, -1))
    g["qg"] = np.ascontiguousarray(inputs["L0_attn_q_ln_g"].reshape(4, 128).T)
    g["qb"] = np.ascontiguousarray(inputs["L0_attn_q_ln_b"].reshape(4, 128).T)
    g["kg"] = np.ascontiguousarray(inputs["L0_attn_kv_ln_g"].reshape(4, 128).T)
    g["kb"] = np.ascontiguousarray(inputs["L0_attn_kv_ln_b"].reshape(4, 128).T)
    g["gate"] = np.ascontiguousarray(
        inputs["L0_ffn_gate_w"].T.reshape(16, 128, 8).transpose(1, 0, 2).reshape(128, -1))

    def tile_w12(w):
        E_, I_, C_ = w.shape
        nic = I_ // 128
        wt = w.transpose(0, 2, 1)
        out = wt.reshape(E_, 16, 128, nic, 128).transpose(0, 3, 2, 1, 4)
        return np.ascontiguousarray(out.reshape(E_ * nic, 128, 16 * 128))

    def tile_w3(w):
        E_, C_, I_ = w.shape
        nic = I_ // 128
        wt = w.transpose(0, 2, 1)
        return np.ascontiguousarray(wt.reshape(E_ * nic, 128, C_))

    g["w1t"] = tile_w12(inputs["L0_ffn_w1"])
    g["w2t"] = tile_w12(inputs["L0_ffn_w2"])
    g["w3t"] = tile_w3(inputs["L0_ffn_w3"])
    g["sw1t"] = tile_w12(inputs["L0_ffn_sw1"][None])
    g["sw2t"] = tile_w12(inputs["L0_ffn_sw2"][None])
    g["sw3t"] = tile_w3(inputs["L0_ffn_sw3"][None])
    g["identb"] = np.eye(128, dtype=ml_dtypes.bfloat16)
    g["identf"] = np.eye(128, dtype=np.float32)
    i = np.arange(128)[:, None]
    c = np.arange(896)[None, :]
    g["masks"] = np.ascontiguousarray(
        np.where(c - 384 <= i, 0.0, NEGBIG).astype(np.float32))
    return g


_BUILD_CACHE = {}


def kernel(**inputs):
    x = inputs["L0_attn_x"]
    g = _relayout(inputs)
    dbg = tuple(sorted(kernel.debug)) if getattr(kernel, "debug", None) else ()
    limit = getattr(kernel, "limit", 99)
    key = (dbg, limit)
    if key not in _BUILD_CACHE:
        with allow_ici_all_to_all():
            _BUILD_CACHE[key] = build_program(dbg, limit)
    b = _BUILD_CACHE[key]
    in_maps = []
    for r in range(NCORE):
        chunks = [x[0, 256 * r:256 * (r + 1)], x[0, 256 * (15 - r):256 * (16 - r)],
                  x[1, 256 * r:256 * (r + 1)], x[1, 256 * (15 - r):256 * (16 - r)]]
        m = dict(g)
        m["x_own"] = np.ascontiguousarray(np.concatenate(chunks, axis=0))
        in_maps.append(m)
    res = run_bass_kernel_spmd(b.nc, in_maps, core_ids=list(range(NCORE)))
    outp = np.zeros((2, 4096, 2048), dtype=np.float32)
    for r in range(NCORE):
        o = res.results[r]["out"]
        outp[0, 256 * r:256 * (r + 1)] = o[0:256]
        outp[0, 256 * (15 - r):256 * (16 - r)] = o[256:512]
        outp[1, 256 * r:256 * (r + 1)] = o[512:768]
        outp[1, 256 * (15 - r):256 * (16 - r)] = o[768:1024]
    kernel.last_results = res
    return outp


kernel.debug = ()
